# version 25
# baseline (speedup 1.0000x reference)
import numpy as np
import contextlib
import concourse.bass as bass
import concourse.mybir as mybir

F32 = mybir.dt.float32
BF16 = mybir.dt.bfloat16
I32 = mybir.dt.int32
AF = mybir.ActivationFunctionType
OP = mybir.AluOpType
AX = mybir.AxisListType

D, S = 2048, 8192
TOK, TT = 1024, 2048
NH, NKV, HD = 16, 4, 128
WIN = 1024
HS, HR, NE = 8192, 1024, 8
SCALE = float(HD) ** -0.5
DC = D // 128
NEG = -1e9
KBy = 1024

SB_BASE = 16512
R_A, R_B, R_C, R_D = SB_BASE, SB_BASE + 65536, SB_BASE + 98304, SB_BASE + 114688
R_E, R_W, R_T, R_S = SB_BASE + 131328, SB_BASE + 164096, SB_BASE + 188672, SB_BASE + 205056
W_SLOTS, W_ELEMS = 3, 4096

SEM_NAMES = (["sT", "sV", "sS", "sG", "sXB", "sConst", "sXF", "sSg0", "sSg1", "sXB0", "sXB1", "sXB2", "sXB3",
              "sSp0", "sSp1", "sR0", "sR1", "sO0", "sO1", "sDbg"]
             + [f"sW{i}" for i in range(W_SLOTS)])


def declare(nc, dbg=()):
    H = {}
    dt = nc.dram_tensor
    H["xT"] = dt("xT", [D, TT], F32, kind="ExternalInput").ap()
    H["tabCin"] = dt("tabCin", [128, TT], F32, kind="ExternalInput").ap()
    H["tabSin"] = dt("tabSin", [128, TT], F32, kind="ExternalInput").ap()
    H["kbc"] = dt("kbc", [128, TT // 128], F32, kind="ExternalInput").ap()
    H["qnw"] = dt("qnw", [128, 1], F32, kind="ExternalInput").ap()
    H["knw"] = dt("knw", [128, 1], F32, kind="ExternalInput").ap()
    H["ebias"] = dt("ebias", [128, NE], F32, kind="ExternalInput").ap()
    H["wqT"] = dt("wqT", [NH, 128, D], F32, kind="ExternalInput").ap()
    H["wkT"] = dt("wkT", [NKV, 128, D], F32, kind="ExternalInput").ap()
    H["wvT"] = dt("wvT", [D, NKV * HD], F32, kind="ExternalInput").ap()
    H["woT"] = dt("woT", [DC, 128, D], F32, kind="ExternalInput").ap()
    H["rwT"] = dt("rwT", [D, NE], F32, kind="ExternalInput").ap()
    H["sw1T"] = dt("sw1T", [HS // 128, 128, D], F32, kind="ExternalInput").ap()
    H["sw2T"] = dt("sw2T", [HS // 128, 128, D], F32, kind="ExternalInput").ap()
    H["sw3T"] = dt("sw3T", [DC, 128, HS], F32, kind="ExternalInput").ap()
    H["w12T"] = dt("w12T", [NE, 16, 128, D], F32, kind="ExternalInput").ap()
    H["w3T"] = dt("w3T", [DC, NE, 128, HR], F32, kind="ExternalInput").ap()
    H["out"] = dt("out", [D, TOK], F32, kind="ExternalOutput").ap()
    H["spso"] = dt("spso", [D, TOK], F32, kind="Internal").ap()

    A = nc.alloc_sbuf_tensor_at
    nc.alloc_sbuf_tensor("arena", [128, 52384], F32)
    H["xbf"] = A("xbf", [128, DC * TT], BF16, offset=R_A)
    H["xf32"] = A("xf32", [128, DC * TOK], F32, offset=R_A)
    H["hbufA"] = A("hbufA", [128, 32 * TOK], BF16, offset=R_A)
    H["scr1"] = A("scr1", [128, TT], F32, offset=R_B)
    H["scr2"] = A("scr2", [128, TT], F32, offset=R_B + 8 * KBy)
    H["scrI"] = A("scrI", [128, TT], I32, offset=R_B + 16 * KBy)
    H["scr3"] = A("scr3", [128, TT], F32, offset=R_B + 24 * KBy)
    H["qhat"] = A("qhat", [128, NH * TOK], BF16, offset=R_B)
    H["hbufB1"] = A("hbufB1", [128, 16 * TOK], BF16, offset=R_B)
    H["khat"] = A("khat", [128, NKV * TT], BF16, offset=R_C)
    H["hbufB2"] = A("hbufB2", [128, 8 * TOK], BF16, offset=R_C)
    H["vtm"] = A("vtm", [128, NKV * 16 * 130], BF16, offset=R_D)
    H["hbufB3"] = A("hbufB3", [128, 8 * TOK], BF16, offset=R_D)
    H["wvsb"] = A("wvsb", [128, DC * (NKV * HD)], BF16, offset=R_E)
    H["rsb"] = A("rsb", [128, 2 * 512], F32, offset=R_E + 16 * KBy)
    H["ropet"] = A("ropet", [128, 512], BF16, offset=R_E + 16 * KBy)
    H["sqb1"] = A("sqb1", [128, 2 * 512], BF16, offset=R_E + 20 * KBy)
    H["sgb2"] = A("sgb2", [128, 2 * 512], BF16, offset=R_E + 22 * KBy)
    H["srow1"] = A("srow1", [1, 512], F32, offset=R_E + 24 * KBy)
    H["rstdb1"] = A("rstdb1", [1, 2 * 512], BF16, offset=R_E + 26 * KBy)
    H["qbf1"] = A("qbf1", [128, 3 * 512], BF16, offset=R_E + 28 * KBy)
    H["pring"] = A("pring", [128, 12 * 512], BF16, offset=R_E)
    H["otmp"] = A("otmp", [128, 2 * 128], BF16, offset=R_E + 12 * KBy)
    H["mrow"] = A("mrow", [1, 1024], BF16, offset=R_E + 12 * KBy + 512)
    H["mcol"] = A("mcol", [128, 4], F32, offset=R_E + 15 * KBy)
    H["mcolb"] = A("mcolb", [128, 2], BF16, offset=R_E + 15 * KBy + 256)
    H["x2bf"] = A("x2bf", [128, DC * TOK], BF16, offset=R_E)
    H["x3bf"] = A("x3bf", [128, DC * TOK], BF16, offset=R_E)
    H["wring"] = A("wring", [128, W_SLOTS * W_ELEMS], BF16, offset=R_W)
    H["tabC"] = A("tabC", [128, TT], F32, offset=R_T)
    H["tabS"] = A("tabS", [128, TT], F32, offset=R_T + 8 * KBy)
    H["x1bf"] = A("x1bf", [128, 2 * 512], BF16, offset=R_T)
    H["sqb3"] = A("sqb3", [128, 2 * 512], BF16, offset=R_T + 2 * KBy)
    H["silub"] = A("silub", [128, 2 * 512], BF16, offset=R_T)
    H["sptmpSO"] = A("sptmpSO", [128, 2 * 512], F32, offset=R_T + 2 * KBy)
    H["sqtmp5"] = A("sqtmp5", [128, 4 * 512], BF16, offset=R_T)
    H["rltmpSO"] = A("rltmpSO", [128, 2 * 512], BF16, offset=R_T + 14 * KBy)
    H["lnrowB"] = A("lnrowB", [1, 2048], BF16, offset=R_T + 10 * KBy)
    H["outtmp2"] = A("outtmp2", [128, 1024], F32, offset=R_T + 2 * KBy)
    H["srow"] = A("srow", [1, 3 * 512], F32, offset=R_T + 4 * KBy)
    off = R_S

    def sm(name, shape, dtp):
        nonlocal off
        t = A(name, shape, dtp, offset=off)
        elems = int(np.prod(shape[1:]))
        sz = elems * (2 if dtp == BF16 else 4)
        off += (sz + 31) // 32 * 32
        return t

    H["ident"] = sm("ident", [128, 128], BF16)
    H["perm"] = sm("perm", [128, 128], BF16)
    H["ones_col"] = sm("ones_col", [128, 1], BF16)
    H["ones_row"] = sm("ones_row", [1, 128], BF16)
    H["mask_c_qm"] = sm("mask_c_qm", [128, 128], BF16)
    H["mask_e_qm"] = sm("mask_e_qm", [128, 128], BF16)
    H["mask_c_km"] = sm("mask_c_km", [128, 128], BF16)
    H["mask_e_km"] = sm("mask_e_km", [128, 128], BF16)
    H["combsb"] = sm("combsb", [8, TOK], BF16)
    H["selsb"] = sm("selsb", [128, 4 * NE], F32)
    H["rtrows"] = sm("rtrows", [128, 8], F32)
    H["rwsb"] = sm("rwsb", [128, DC * NE], BF16)
    H["qnwsb"] = sm("qnwsb", [128, 1], F32)
    H["knwsb"] = sm("knwsb", [128, 1], F32)
    H["ebsb"] = sm("ebsb", [128, NE], F32)
    H["cols"] = sm("cols", [128, 8], F32)
    H["v11"] = sm("v11", [1, 8], F32)
    H["epsv"] = sm("epsv", [1, 4], F32)
    H["kbcsb"] = sm("kbcsb", [128, TT // 128], F32)
    H["cmb8"] = sm("cmb8", [128, NE], BF16)
    H["sgb"] = sm("sgb", [128, 512], BF16)
    H["esel"] = sm("esel", [8, NE * 128], BF16)
    assert off <= SB_BASE + 212863, f"smalls overflow {off}"
    H["outtmp"] = A("outtmp", [128, 2 * 512], F32, offset=R_S)


    H["psA"] = nc.alloc_psum_tensor("psA", [128, 7 * 512], F32)
    H["psS"] = nc.alloc_psum_tensor("psS", [128, 1024], BF16)

    H["_dbg"] = {}
    for name, shape, dtp in [
        ("d_tabC", [128, TT], F32), ("d_tabS", [128, TT], F32),
        ("d_khat", [128, NKV * TT], BF16), ("d_qhat", [128, NH * TOK], BF16),
        ("d_vtm", [128, NKV * 16 * 130], BF16),
        ("d_attnout", [128, NH * TOK], BF16),
        ("d_x2bf", [128, DC * TOK], BF16),
        ("d_comb", [8, TOK], BF16),
        ("d_hbuf", [128, 32 * TOK], BF16),
        ("d_so", [D, TOK], F32),
        ("d_x3bf", [128, DC * TOK], BF16),
        ("d_cols", [128, 8], F32),
        ("d_ang", [128, TT], F32),
        ("d_kf", [128, TT], F32),
        ("d_r", [128, TT], F32),
    ]:
        if name in dbg:
            H["_dbg"][name] = dt(name, shape, dtp, kind="ExternalOutput").ap()
    return H


class Plan:
    def __init__(self, nc, ee, H, sems):
        self.nc, self.ee, self.H, self.sems = nc, ee, H, sems
        self.n = {k: 0 for k in SEM_NAMES}
        self.TE, self.VE, self.SC = nc.tensor, nc.vector, nc.scalar
        self.GP, self.SY = nc.gpsimd, nc.sync
        self.bank_rec = [None] * 7
        self.bank_busy = [False] * 7
        self.bank_next = 0
        self.wslot_gen = [0] * W_SLOTS
        self.wslot_rec = [None] * W_SLOTS
        self.wslot_next = 0

    def _eng(self, tag):
        return {"te": self.TE, "ve": self.VE, "sc": self.SC,
                "gp": self.GP, "sy": self.SY}[tag]

    def _waits(self, tag, waits):
        if self.ee == tag:
            for (s, v) in waits:
                if v > 0:
                    self._eng(tag).wait_ge(self.sems[s], v)

    def _fin(self, tag, inst, incs):
        for (s, k) in incs:
            self.n[s] += k
            if inst is not None:
                inst.then_inc(self.sems[s], k)

    def mm(self, out, lhsT, rhs, start, stop, waits=(), incs=()):
        self._waits("te", waits)
        inst = self.TE.matmul(out, lhsT, rhs, start=start, stop=stop) \
            if self.ee == "te" else None
        self._fin("te", inst, incs)

    def tr(self, out, in_, waits=(), incs=()):
        self._waits("te", waits)
        inst = self.TE.transpose(out, in_, self.H["ident"][:, 0:128]) \
            if self.ee == "te" else None
        self._fin("te", inst, incs)

    def _comp(self, tag, fn, args, kwargs, waits, incs, drain):
        self._waits(tag, waits)
        inst = getattr(self._eng(tag), fn)(*args, **kwargs) if self.ee == tag else None
        self._fin(tag, inst, incs)
        if drain and self.ee == tag:
            self._eng(tag).drain()

    def ve(self, fn, *args, waits=(), incs=(), drain=False, **kw):
        self._comp("ve", fn, args, kw, waits, incs, drain)

    def sc(self, fn, *args, waits=(), incs=(), drain=False, **kw):
        self._comp("sc", fn, args, kw, waits, incs, drain)

    def gp(self, fn, *args, waits=(), incs=(), drain=True, **kw):
        self._comp("gp", fn, args, kw, waits, incs, drain)

    def dma(self, tag, out, in_, sem, waits=()):
        self._waits(tag, waits)
        inst = self._eng(tag).dma_start(out=out, in_=in_) if self.ee == tag else None
        self._fin(tag, inst, ((sem, 16),))

    def tap(self, name, src_ap, waits):
        if name in self.H["_dbg"]:
            self.dma("sy", self.H["_dbg"][name][:, :], src_ap, "sDbg", waits=waits)

    def ps_alloc(self, nbank=1):
        for _ in range(9):
            b = self.bank_next
            if b + nbank > 7:
                b = 0
            if all(not self.bank_busy[b + i] for i in range(nbank)):
                waits = []
                for i in range(nbank):
                    self.bank_busy[b + i] = True
                    if self.bank_rec[b + i]:
                        waits.append(self.bank_rec[b + i])
                self.bank_next = (b + nbank) % 7
                return {"b": b, "nb": nbank,
                        "ap": self.H["psA"][:, b * 512:(b + nbank) * 512],
                        "waits": waits}
            self.bank_next = (self.bank_next + 1) % 7
        raise RuntimeError(f"psum full busy={self.bank_busy} next={self.bank_next}")

    def ps_free(self, reg, sem):
        v = self.n[sem]
        for i in range(reg["nb"]):
            self.bank_busy[reg["b"] + i] = False
            self.bank_rec[reg["b"] + i] = (sem, v)

    def wload(self, dram_ap, nelem, nshape=None):
        s = self.wslot_next
        self.wslot_next = (s + 1) % W_SLOTS
        waits = [self.wslot_rec[s]] if self.wslot_rec[s] else []
        ap = self.H["wring"][:, s * W_ELEMS: s * W_ELEMS + nelem]
        dst = ap if nshape is None else ap.rearrange("p (a b) -> p a b", a=nshape[0])
        self.dma("gp", dst, dram_ap, f"sW{s}", waits=waits)
        self.wslot_gen[s] += 1
        return {"ap": ap, "ready": (f"sW{s}", 16 * self.wslot_gen[s]), "s": s}

    def wfree(self, h, val=None):
        self.wslot_rec[h["s"]] = ("sT", self.n["sT"] if val is None else val)


def walk(p: Plan):
    H = p.H
    setup(p)
    phase1(p)
    phase2(p)
    phase3(p)
    phase4(p)
    phase5(p)
    p._waits("sy", [("sO0", p.n["sO0"]), ("sO1", p.n["sO1"]),
                    ("sDbg", p.n["sDbg"])])


def setup(p: Plan):
    H = p.H
    p.dma("gp", H["kbcsb"][:, :], H["kbc"][:, :], "sConst")
    p.dma("gp", H["qnwsb"][:, :], H["qnw"][:, :], "sConst")
    p.dma("gp", H["knwsb"][:, :], H["knw"][:, :], "sConst")
    p.dma("gp", H["ebsb"][:, :], H["ebias"][:, :], "sConst")
    for kc in range(DC):
        p.dma("gp", H["rwsb"][:, kc * NE:(kc + 1) * NE],
              H["rwT"][kc * 128:(kc + 1) * 128, :], "sConst")
    xbf4 = H["xbf"].ap().rearrange("p (k t) -> p k t", k=DC)
    xT4 = H["xT"][:, :].rearrange("(k p) t -> p k t", p=128)
    for qtr in range(4):
        p.dma("gp", xbf4[:, :, qtr * 512:(qtr + 1) * 512],
              xT4[:, :, qtr * 512:(qtr + 1) * 512], f"sXB{qtr}")
    p.dma("gp", H["wvsb"].ap().rearrange("p (k o) -> p k o", k=DC),
          H["wvT"][:, :].rearrange("(k p) o -> p k o", p=128), "sConst")

    g = p.gp
    g("memset", H["ident"][:, :], 1.0, incs=(("sG", 1),))
    g("affine_select", H["ident"][:, :], H["ident"][:, :], [[1, 128]],
      OP.is_equal, 0.0, base=0, channel_multiplier=-1,
      waits=[("sG", p.n["sG"])], incs=(("sG", 1),))
    g("memset", H["perm"][:, :], 1.0, incs=(("sG", 1),))
    g("affine_select", H["perm"][:, :], H["perm"][:, :], [[1, 128]],
      OP.is_equal, 0.0, base=64, channel_multiplier=-1,
      waits=[("sG", p.n["sG"])], incs=(("sG", 1),))
    tmp = H["sqb1"][:, 0:128]
    g("memset", tmp, 1.0, incs=(("sG", 1),))
    g("affine_select", tmp, tmp, [[1, 128]], OP.is_equal, 0.0,
      base=-64, channel_multiplier=-1,
      waits=[("sG", p.n["sG"])], incs=(("sG", 1),))
    p.ve("tensor_add", H["perm"][:, :], H["perm"][:, :], tmp,
         waits=[("sG", p.n["sG"])], incs=(("sV", 1),))
    for name, base, cm, step in [
        ("mask_c_qm", 0, 1, -1),
        ("mask_e_qm", -1, -1, 1),
        ("mask_c_km", 0, -1, 1),
        ("mask_e_km", -1, 1, -1),
    ]:
        g("memset", H[name][:, :], 0.0, incs=(("sG", 1),))
        g("affine_select", H[name][:, :], H[name][:, :], [[step, 128]],
          OP.is_ge, NEG, base=base, channel_multiplier=cm,
          waits=[("sG", p.n["sG"])], incs=(("sG", 1),))
    g("memset", H["esel"][:, :], 1.0, incs=(("sG", 1),))
    for e in range(NE):
        g("affine_select", H["esel"][0:8, e * 128:(e + 1) * 128],
          H["esel"][0:8, e * 128:(e + 1) * 128], [[0, 128]],
          OP.is_equal, 0.0, base=-e, channel_multiplier=1,
          waits=[("sG", p.n["sG"])] if e == 0 else (), incs=(("sG", 1),))
    g("memset", H["ones_col"][:, :], 1.0, incs=(("sG", 1),))
    g("memset", H["ones_row"][:, :], 1.0, incs=(("sG", 1),))
    g("memset", H["vtm"][:, :], 1.0, incs=(("sG", 1),))
    GDONE = p.n["sG"]

    p.dma("gp", H["tabC"][:, :], H["tabCin"][:, :], "sConst")
    p.dma("gp", H["tabS"][:, :], H["tabSin"][:, :], "sConst")
    p.ve("memset", H["epsv"][0:1, 0:1], 1e-6, incs=(("sV", 1),))
    p.ve("memset", H["epsv"][0:1, 1:2], 1e-5, incs=(("sV", 1),))
    p.ve("memset", H["cols"][:, 0:1], 0.0078125, incs=(("sV", 1),))
    p.sc("activation", H["cols"][:, 1:2], H["cols"][:, 0:1], AF.Sqrt,
         waits=[("sV", p.n["sV"])], incs=(("sS", 1),))
    p.SETUP = dict(sV=p.n["sV"], sS=p.n["sS"], sG=p.n["sG"],
                   sConst=p.n["sConst"], sXB=p.n["sXB"])


def _p1_pipe(p: Plan, which, hook=None, qdone=None):
    H = p.H
    st = p.SETUP
    xw = [("sConst", st["sConst"])]
    if which == "k":
        items = [(b, s4, s4 * 512, 4) for b in range(NKV) for s4 in range(4)]
        wdram, wcol, outbuf, outTT = H["wkT"], H["knwsb"], H["khat"], TT
        xq = lambda tok0: [(f"sXB{tok0 // 512}", 16)]
    else:
        items = [(b, s2, 1024 + s2 * 512, 2) for b in range(NH) for s2 in range(2)]
        wdram, wcol, outbuf, outTT = H["wqT"], H["qnwsb"], H["qhat"], TOK
        xq = lambda tok0: [(f"sXB{tok0 // 512}", 16)]
    n = len(items)
    state = {}
    sq_rec = [None] * 2
    qw_rec = [None] * 3
    sr_rec = [None] * 2
    rb_rec = [None] * 2
    sg_rec = [None, None]
    wt = None

    def S0(i):
        nonlocal wt
        b, sl, tok0, nsl = items[i]
        if sl == 0:
            wt = p.wload(wdram[b], D)
        reg = p.ps_alloc(1)
        for kc in range(DC):
            w = ()
            if kc == 0:
                w = reg["waits"] + [wt["ready"]] + ((xw + xq(tok0)) if i < nsl else [])
            elif kc == DC - 1 and sl == nsl - 1:
                w = [wt["ready"]]
            p.mm(reg["ap"][:, 0:512], wt["ap"][:, kc * 128:(kc + 1) * 128],
                 H["xbf"][:, kc * TT + tok0: kc * TT + tok0 + 512],
                 start=(kc == 0), stop=(kc == DC - 1), waits=w,
                 incs=(("sT", 1),) if kc == DC - 1 else ())
        if sl == nsl - 1:
            p.wfree(wt, p.n["sT"])
        PROJ = p.n["sT"]
        sq = H["sqb1"][:, (i % 2) * 512:(i % 2) * 512 + 512]
        w1 = [("sT", PROJ)] + ([sq_rec[i % 2]] if sq_rec[i % 2] else [])
        p.sc("activation", sq, reg["ap"][:, 0:512], AF.Square, waits=w1,
             incs=(("sS", 1),))
        qw = H["qbf1"][:, (i % 3) * 512:(i % 3) * 512 + 512]
        w2 = ([qw_rec[i % 3]] if qw_rec[i % 3] else [])
        p.sc("activation", qw, reg["ap"][:, 0:512], AF.Copy, scale=wcol[:, 0:1],
             waits=w2, incs=(("sS", 1),))
        QW = p.n["sS"]
        p.ps_free(reg, "sS")
        sg0 = (i % 2) * 512
        sem = f"sSg{i % 2}"
        dw = [("sS", QW)] + ([sg_rec[i % 2]] if sg_rec[i % 2] else [])
        p.dma("sy", H["sgb2"][0:64, sg0:sg0 + 512],
              qw[64:128, :], sem, waits=dw)
        p.dma("sy", H["sgb2"][64:128, sg0:sg0 + 512],
              qw[0:64, :], sem)
        state[i] = {"qw": qw, "sq": sq, "sqv": p.n["sS"] - 1, "tok0": tok0,
                    "b": b, "sg": (sem, p.n[sem]), "sgslot": sg0}

    def S1(i):
        stt = state[i]
        ttok = stt["tok0"]
        sg = H["sgb2"][:, stt["sgslot"]:stt["sgslot"] + 512]
        t1 = H["ropet"][:, 0:512]
        p.ve("tensor_mul", t1, stt["qw"], H["tabC"][:, ttok:ttok + 512],
             waits=[("sConst", p.SETUP["sConst"]), ("sS", stt["sqv"] + 1)],
             incs=(("sV", 1),))
        p.ve("tensor_mul", sg, sg, H["tabS"][:, ttok:ttok + 512],
             waits=[stt["sg"]], incs=(("sV", 1),), drain=True)
        p.ve("tensor_add", stt["qw"], t1, sg, incs=(("sV", 1),), drain=True)
        sg_rec[i % 2] = ("sV", p.n["sV"])
        stt["uv"] = p.n["sV"]
        ssreg = p.ps_alloc(1)
        p.mm(ssreg["ap"][0:1, 0:512], H["ones_col"][:, :], stt["sq"], True, True,
             waits=ssreg["waits"] + [("sS", stt["sqv"])], incs=(("sT", 1),))
        sq_rec[i % 2] = ("sT", p.n["sT"])
        sr = H["srow1"][0:1, 0:512]
        w1 = [("sT", p.n["sT"])] + ([sr_rec[0]] if sr_rec[0] else [])
        p.sc("activation", sr, ssreg["ap"][0:1, 0:512], AF.Sqrt,
             scale=1.0 / HD, bias=H["epsv"][0:1, 0:1], waits=w1, incs=(("sS", 1),))
        p.ps_free(ssreg, "sS")
        rb = H["rstdb1"][0:1, (i % 2) * 512:(i % 2) * 512 + 512]
        w2 = [("sS", p.n["sS"])] + ([rb_rec[i % 2]] if rb_rec[i % 2] else [])
        p.ve("reciprocal", rb, sr, waits=w2, incs=(("sV", 1),))
        sr_rec[0] = ("sV", p.n["sV"])
        stt["rb"] = rb
        stt["rbv"] = p.n["sV"]

    def S2(i):
        stt = state[i]
        breg = p.ps_alloc(1)
        p.mm(breg["ap"][:, 0:512], H["ones_row"][0:1, 0:128], stt["rb"], True, True,
             waits=breg["waits"] + [("sV", stt["rbv"])], incs=(("sT", 1),))
        rb_rec[i % 2] = ("sT", p.n["sT"])
        b = stt["b"]
        out0 = b * outTT + (stt["tok0"] - (1024 if which == "q" else 0))
        p.ve("tensor_mul", outbuf[:, out0:out0 + 512], stt["qw"],
             breg["ap"][:, 0:512], waits=[("sT", p.n["sT"])], incs=(("sV", 1),))
        qw_rec[i % 3] = ("sV", p.n["sV"])
        if qdone is not None and i % 2 == 1:
            qdone[i // 2] = p.n["sV"]
        p.ps_free(breg, "sV")
        del state[i]

    for step in range(n + 2):
        if step < n:
            S0(step)
        if hook is not None:
            hook(step)
        if step >= 1 and step - 1 < n:
            S1(step - 1)
        if step >= 2:
            S2(step - 2)


def phase1(p: Plan):
    H = p.H
    st = p.SETUP
    _p1_pipe(p, "k")
    p.tap("d_khat", H["khat"][:, :], [("sV", p.n["sV"])])
    for kt in range(16):
        reg = p.ps_alloc(1)
        for kc in range(DC):
            p.mm(reg["ap"][:, 0:NKV * HD],
                 H["xbf"][:, kc * TT + kt * 128: kc * TT + (kt + 1) * 128],
                 H["wvsb"][:, kc * 512:(kc + 1) * 512],
                 start=(kc == 0), stop=(kc == DC - 1),
                 waits=(reg["waits"] + [("sConst", st["sConst"]),
                        (f"sXB{kt // 4}", 16)]) if kc == 0 else (),
                 incs=(("sT", 1),) if kc == DC - 1 else ())
        out = H["vtm"].ap().rearrange("p (t h e) -> p t h e", t=16, h=4, e=130)
        p.ve("tensor_copy", out[:, kt, :, 0:128],
             reg["ap"].rearrange("p (h e) -> p h e", h=4)[:, :, 0:128],
             waits=[("sT", p.n["sT"]), ("sG", st["sG"])], incs=(("sV", 1),))
        p.ps_free(reg, "sV")
    p.tap("d_vtm", H["vtm"][:, :], [("sV", p.n["sV"])])
    p.P1 = dict(sV=p.n["sV"], sT=p.n["sT"])


def phase2(p: Plan):
    H = p.H
    st2 = {"pslot_rec": [None] * 12, "oslot_rec": [None] * 2, "tslot_rec": [None] * 2,
           "mslot_rec": [None] * 4, "pnext": 0, "mrow_rec": [None, None],
           "mcb_rec": [None, None], "rcp_rec": [None, None], "qready": 0}
    groups = [(h, qg) for h in range(NH) for qg in range(2)]
    qmout = {}
    qdone = {}
    gidx = [0]

    def tick():
        gi = gidx[0]
        if gi > len(groups):
            return
        if gi < len(groups):
            st2["qready"] = qdone[groups[gi][0]]
            qmout[gi] = _p2_qm(p, groups[gi], gi, st2)
        if gi >= 1:
            _p2_km(p, groups[gi - 1], gi - 1, st2, qmout[gi - 1])
        gidx[0] += 1

    _p1_pipe(p, "q", hook=None, qdone=qdone)
    p.tap("d_qhat", H["qhat"][:, :], [("sV", p.n["sV"])])
    while gidx[0] <= len(groups):
        tick()
    p.tap("d_attnout", H["qhat"][:, :], [("sV", p.n["sV"])])
    p.P2 = dict(sV=p.n["sV"], sT=p.n["sT"], sS=p.n["sS"])


def _p2_qm(p: Plan, g, gi, st2):
    H = p.H
    h, qg = g
    q0 = qg * 512
    kv = h // 4
    par = gi % 2
    regs = [None] * 4
    red_v = [None] * 4
    for t in range(5):
        if t < 4:
            qoff = q0 + t * 128
            k0 = (4 * qg + t) * 128
            lhs_q = H["qhat"][:, h * TOK + qoff: h * TOK + qoff + 128]
            reg = p.ps_alloc(3)
            regs[t] = reg
            first_w = reg["waits"] + ([("sV", st2["qready"])] if t == 0 else [])
            for pi, (pk, pw) in enumerate([(0, 512), (512, 512), (1024, 128)]):
                p.mm(reg["ap"][:, pk:pk + pw], lhs_q,
                     H["khat"][:, kv * TT + k0 + pk: kv * TT + k0 + pk + pw],
                     start=True, stop=True,
                     waits=first_w if pi == 0 else (),
                     incs=(("sT", 1),) if pi == 2 else ())
            mw = [("sT", p.n["sT"])]
            if st2["mcb_rec"][t % 2]:
                mw.append(st2["mcb_rec"][t % 2])
            p.ve("tensor_reduce", H["mcolb"][:, t % 2: t % 2 + 1],
                 reg["ap"][:, 0:1152], AX.X, OP.max, negate=True,
                 waits=mw, incs=(("sV", 1),))
            red_v[t] = p.n["sV"]
            p.ps_free(reg, "sV")
        if t >= 1:
            tp = t - 1
            tw = [("sV", red_v[tp])]
            if st2["mslot_rec"][tp]:
                tw.append(st2["mslot_rec"][tp])
            p.tr(H["psS"][0:1, tp * 128:(tp + 1) * 128],
                 H["mcolb"][:, tp % 2: tp % 2 + 1], waits=tw, incs=(("sT", 1),))
            st2["mcb_rec"][tp % 2] = ("sT", p.n["sT"])
            st2["mslot_rec"][tp] = None
    cw = [("sT", p.n["sT"])]
    if st2["mrow_rec"][par]:
        cw.append(st2["mrow_rec"][par])
    p.ve("tensor_copy", H["mrow"][0:1, par * 512:(par + 1) * 512],
         H["psS"][0:1, 0:512], waits=cw, incs=(("sV", 1),))
    for t in range(4):
        st2["mslot_rec"][t] = ("sV", p.n["sV"])
    return p.n["sV"]


def _p2_km(p: Plan, g, gi, st2, MROW_READY):
    H = p.H
    h, qg = g
    q0 = qg * 512
    kv = h // 4
    par = gi % 2
    oreg = p.ps_alloc(2)
    OOFF = [0, 132, 264, 512]
    av_done = [None] * 4
    exp_done = [None] * 12
    for jj in range(12):
        t_lo, t_hi = max(0, jj - 8), min(3, jj)
        W = (t_hi - t_lo + 1) * 128
        kb_i = 4 * qg + jj
        reg = p.ps_alloc(1)
        p.mm(reg["ap"][:, 0:W],
             H["khat"][:, kv * TT + kb_i * 128: kv * TT + (kb_i + 1) * 128],
             H["qhat"][:, h * TOK + q0 + t_lo * 128:
                       h * TOK + q0 + t_lo * 128 + W],
             start=True, stop=False,
             waits=reg["waits"] + ([("sV", MROW_READY)] if jj == 0 else []))
        if jj >= 8:
            c0 = (jj - 8 - t_lo) * 128
            p.mm(reg["ap"][:, c0:c0 + 128], H["ident"][:, :],
                 H["mask_c_km"][:, :], start=False, stop=False)
        if jj <= 3:
            c0 = (jj - t_lo) * 128
            p.mm(reg["ap"][:, c0:c0 + 128], H["ident"][:, :],
                 H["mask_e_km"][:, :], start=False, stop=False)
        p.mm(reg["ap"][:, 0:W], H["ones_row"][0:1, 0:128],
             H["mrow"][0:1, par * 512 + t_lo * 128: par * 512 + t_lo * 128 + W],
             start=False, stop=True, incs=(("sT", 1),))
        if jj == 11:
            st2["mrow_rec"][par] = ("sT", p.n["sT"])
        ew = [("sT", p.n["sT"])]
        if st2["pslot_rec"][jj]:
            ew.append(st2["pslot_rec"][jj])
        pslot = H["pring"][:, jj * 512: jj * 512 + W]
        p.sc("activation", pslot, reg["ap"][:, 0:W], AF.Exp,
             scale=H["cols"][:, 1:2], bias=H["kbcsb"][:, kb_i:kb_i + 1],
             waits=ew, incs=(("sS", 1),))
        p.ps_free(reg, "sS")
        exp_done[jj] = p.n["sS"]
    for jj in range(12):
        t_lo, t_hi = max(0, jj - 8), min(3, jj)
        kb_i = 4 * qg + jj
        for t in range(t_lo, t_hi + 1):
            stt = (t == 0 and jj == 0) or (t == 3 and jj == 3)
            sp = (t == 2 and jj == 10) or (t == 3 and jj == 11)
            p.mm(oreg["ap"][:, OOFF[t]:OOFF[t] + 129],
                 H["pring"][:, jj * 512 + (t - t_lo) * 128:
                             jj * 512 + (t - t_lo) * 128 + 128],
                 H["vtm"][:, (kb_i * NKV + kv) * 130:
                          (kb_i * NKV + kv) * 130 + 129],
                 start=stt, stop=sp,
                 waits=([("sS", exp_done[jj])] + (oreg["waits"] if jj == 0 and t == 0 else []))
                 if t == t_lo else (),
                 incs=(("sT", 1),) if jj == t + 8 else ())
            if jj == t + 8:
                av_done[t] = p.n["sT"]
        st2["pslot_rec"][jj] = ("sT", p.n["sT"])
    for t in range(4):
        off = OOFF[t]
        gate = av_done[2] if t < 3 else av_done[3]
        p.ve("reciprocal", H["mcol"][:, t % 2:t % 2 + 1],
             oreg["ap"][:, off + 128:off + 129],
             waits=[("sT", gate)] + ([st2["rcp_rec"][t % 2]] if st2["rcp_rec"][t % 2] else []),
             incs=(("sV", 1),))
        RCPV = p.n["sV"]
        osl = t % 2
        ow = [("sV", RCPV)]
        if st2["oslot_rec"][osl]:
            ow.append(st2["oslot_rec"][osl])
        otile = H["otmp"][:, osl * 128:(osl + 1) * 128]
        p.sc("mul", otile, oreg["ap"][:, off:off + 128],
             H["mcol"][:, t % 2:t % 2 + 1], waits=ow, incs=(("sS", 1),))
        st2["rcp_rec"][t % 2] = ("sS", p.n["sS"])
        tsl = t % 2
        tw = [("sS", p.n["sS"])]
        if st2["tslot_rec"][tsl]:
            tw.append(st2["tslot_rec"][tsl])
        p.tr(H["psS"][:, 512 + tsl * 128: 512 + (tsl + 1) * 128],
             otile, waits=tw, incs=(("sT", 1),))
        st2["oslot_rec"][osl] = ("sT", p.n["sT"])
        p.ve("tensor_copy",
             H["qhat"][:, h * TOK + q0 + t * 128: h * TOK + q0 + (t + 1) * 128],
             H["psS"][:, 512 + tsl * 128: 512 + (tsl + 1) * 128],
             waits=[("sT", p.n["sT"])], incs=(("sV", 1),))
        st2["tslot_rec"][tsl] = ("sV", p.n["sV"])
    p.ps_free(oreg, "sS")


def phase3(p: Plan):
    H = p.H
    for kc in range(DC):
        p.dma("sy", H["xf32"][:, kc * TOK:(kc + 1) * TOK],
              H["xT"][kc * 128:(kc + 1) * 128, TOK:TT], "sXF",
              waits=[("sT", p.P1["sT"])] if kc == 0 else ())
    XFALL = ("sXF", p.n["sXF"])
    ssum = [p.ps_alloc(1) for _ in range(2)]
    ssq = [p.ps_alloc(1) for _ in range(2)]
    x1_rec = [None, None]
    sq_rec = [None, None]
    x1done = [[None] * 2 for _ in range(DC)]
    pend = []

    def emit_stats(b2, s2b, x1sl, sqsl, sv):
        p.mm(ssum[s2b]["ap"][0:1, 0:512], H["ones_col"][:, :], x1sl,
             start=(b2 == 0), stop=(b2 == DC - 1),
             waits=(ssum[s2b]["waits"] if b2 == 0 else []) + [("sS", sv)],
             incs=(("sT", 1),) if b2 == DC - 1 else ())
        p.mm(ssq[s2b]["ap"][0:1, 0:512], H["ones_col"][:, :], sqsl,
             start=(b2 == 0), stop=(b2 == DC - 1),
             waits=(ssq[s2b]["waits"] if b2 == 0 else []),
             incs=(("sT", 1),) if b2 == DC - 1 else ())
        x1_rec[s2b] = ("sT", p.n["sT"])
        sq_rec[s2b] = ("sT", p.n["sT"])

    for b in range(DC):
        wt = p.wload(H["woT"][b], D)
        for s2 in range(2):
            reg = p.ps_alloc(1)
            for kc in range(DC):
                p.mm(reg["ap"][:, 0:512], wt["ap"][:, kc * 128:(kc + 1) * 128],
                     H["qhat"][:, kc * TOK + s2 * 512: kc * TOK + s2 * 512 + 512],
                     start=(kc == 0), stop=(kc == DC - 1),
                     waits=(reg["waits"] + [wt["ready"], ("sV", p.P2["sV"]),
                             ("sS", p.P2["sS"])])
                     if kc == 0
                     else ([wt["ready"]] if (kc == DC - 1 and s2 == 1) else ()),
                     incs=(("sT", 1),) if kc == DC - 1 else ())
            if s2 == 1:
                p.wfree(wt, p.n["sT"])
            if pend:
                emit_stats(*pend.pop(0))
            xsl = H["xf32"][:, b * TOK + s2 * 512: b * TOK + s2 * 512 + 512]
            p.ve("tensor_add", xsl, reg["ap"][:, 0:512], xsl,
                 waits=[("sT", p.n["sT"]), XFALL], incs=(("sV", 1),), drain=True)
            p.ps_free(reg, "sV")
            x1done[b][s2] = p.n["sV"]
            x1sl = H["x1bf"][:, s2 * 512:(s2 + 1) * 512]
            sqsl = H["sqb3"][:, s2 * 512:(s2 + 1) * 512]
            w1 = [("sV", p.n["sV"])] + ([x1_rec[s2]] if x1_rec[s2] else [])
            if b == 0 and s2 == 0:
                w1 += [("sDbg", p.n["sDbg"])]
            p.sc("copy", x1sl, xsl, waits=w1, incs=(("sS", 1),))
            w2 = [("sV", p.n["sV"])] + ([sq_rec[s2]] if sq_rec[s2] else [])
            p.sc("activation", sqsl, xsl, AF.Square, waits=w2, incs=(("sS", 1),))
            pend.append((b, s2, x1sl, sqsl, p.n["sS"]))
    while pend:
        emit_stats(*pend.pop(0))
    STATS_T = p.n["sT"]
    bregs = []
    A_ = H["srow"][0:1, 0:512]
    B_ = H["srow"][0:1, 512:1024]
    C_ = H["srow"][0:1, 1024:1536]
    for s2 in range(2):
        p.ve("tensor_scalar", A_, ssum[s2]["ap"][0:1, 0:512], 1.0 / D, None,
             OP.mult, waits=[("sT", STATS_T)], incs=(("sV", 1),), drain=True)
        p.ve("tensor_mul", B_, A_, A_, incs=(("sV", 1),), drain=True)
        p.ve("scalar_tensor_tensor", B_, ssq[s2]["ap"][0:1, 0:512], 1.0 / D, B_,
             OP.mult, OP.subtract, incs=(("sV", 1),), drain=True)
        p.sc("activation", C_, B_, AF.Sqrt, bias=H["epsv"][0:1, 1:2],
             waits=[("sV", p.n["sV"])], incs=(("sS", 1),), drain=True)
        p.ve("reciprocal", B_, C_, waits=[("sS", p.n["sS"])], incs=(("sV", 1),),
             drain=True)
        p.ve("tensor_copy", H["lnrowB"][0:1, s2 * 512:(s2 + 1) * 512], B_,
             incs=(("sV", 1),), drain=True)
        p.ve("tensor_mul", C_, A_, B_, incs=(("sV", 1),), drain=True)
        p.ve("tensor_copy", H["lnrowB"][0:1, 1024 + s2 * 512: 1024 + (s2 + 1) * 512],
             C_, incs=(("sV", 1),), drain=True)
        p.ps_free(ssum[s2], "sV")
        p.ps_free(ssq[s2], "sV")
        br = p.ps_alloc(1)
        p.mm(br["ap"][:, 0:512], H["ones_row"][0:1, 0:128],
             H["lnrowB"][0:1, s2 * 512:(s2 + 1) * 512], True, True,
             waits=br["waits"] + [("sV", p.n["sV"])], incs=(("sT", 1),))
        bm = p.ps_alloc(1)
        p.mm(bm["ap"][:, 0:512], H["ones_row"][0:1, 0:128],
             H["lnrowB"][0:1, 1024 + s2 * 512: 1024 + (s2 + 1) * 512], True, True,
             waits=bm["waits"], incs=(("sT", 1),))
        bregs.append((br, bm))
    BC_T = p.n["sT"]
    p.X2DONE = {}
    for b in range(DC):
        for s2 in range(2):
            br, bm = bregs[s2]
            xsl = H["xf32"][:, b * TOK + s2 * 512: b * TOK + s2 * 512 + 512]
            usl = H["x1bf"][:, s2 * 512:(s2 + 1) * 512]
            w1 = [("sT", BC_T)] + ([x1_rec[s2]] if x1_rec[s2] else [])
            p.ve("tensor_mul", usl, xsl, br["ap"][:, 0:512], waits=w1,
                 incs=(("sV", 1),), drain=True)
            x1_rec[s2] = None
            p.ve("tensor_sub", H["x2bf"][:, b * TOK + s2 * 512: b * TOK + s2 * 512 + 512],
                 usl, bm["ap"][:, 0:512], incs=(("sV", 1),))
            x1_rec[s2] = ("sV", p.n["sV"])
            p.X2DONE[(b, s2)] = p.n["sV"]
    for br, bm in bregs:
        p.ps_free(br, "sV")
        p.ps_free(bm, "sV")
    p.tap("d_x2bf", H["x2bf"][:, :], [("sV", p.n["sV"])])
    p.P3 = dict(sV=p.n["sV"], sT=p.n["sT"], sS=p.n["sS"])


def hsl(H, c, tok0, w):
    if c < 32:
        return H["hbufA"][:, c * TOK + tok0: c * TOK + tok0 + w]
    if c < 48:
        return H["hbufB1"][:, (c - 32) * TOK + tok0: (c - 32) * TOK + tok0 + w]
    if c < 56:
        return H["hbufB2"][:, (c - 48) * TOK + tok0: (c - 48) * TOK + tok0 + w]
    return H["hbufB3"][:, (c - 56) * TOK + tok0: (c - 56) * TOK + tok0 + w]


def phase4(p: Plan):
    H = p.H
    GATE = [("sV", p.P3["sV"]), ("sS", p.P3["sS"])]
    first_ps = [True]
    rctx = {}

    def router_group(kt):
        reg = p.ps_alloc(1)
        for kc in range(DC):
            w = []
            if kt == 0:
                w = [("sV", p.X2DONE[(kc, 1)])]
            if kc == 0:
                w = reg["waits"] + w + [("sConst", p.SETUP["sConst"])]
            p.mm(reg["ap"][:, 0:NE],
                 H["x2bf"][:, kc * TOK + kt * 128: kc * TOK + (kt + 1) * 128],
                 H["rwsb"][:, kc * NE:(kc + 1) * NE],
                 start=(kc == 0), stop=(kc == DC - 1), waits=w,
                 incs=(("sT", 1),) if kc == DC - 1 else ())
        sc0 = H["selsb"][:, 0:8]
        p.sc("activation", sc0, reg["ap"][:, 0:NE], AF.Sigmoid,
             waits=[("sT", p.n["sT"])], incs=(("sS", 1),))
        p.ps_free(reg, "sS")
        sel = H["selsb"][:, 8:16]
        p.ve("tensor_add", sel, sc0, H["ebsb"][:, :],
             waits=[("sS", p.n["sS"]), ("sConst", p.SETUP["sConst"])], incs=(("sV", 1),), drain=True)
        p.ve("tensor_reduce", H["rtrows"][:, 0:1], sel, AX.X, OP.max, incs=(("sV", 1),), drain=True)
        eq = H["selsb"][:, 16:24]
        p.ve("tensor_scalar", eq, sel, H["rtrows"][:, 0:1], None, OP.is_equal,
             incs=(("sV", 1),), drain=True)
        p.ve("scalar_tensor_tensor", eq, eq, -1e9, sel, OP.mult, OP.add,
             incs=(("sV", 1),), drain=True)
        p.ve("tensor_reduce", H["rtrows"][:, 1:2], eq, AX.X, OP.max, incs=(("sV", 1),), drain=True)
        tm = H["selsb"][:, 24:32]
        p.ve("tensor_scalar", tm, sel, H["rtrows"][:, 1:2], None, OP.is_ge,
             incs=(("sV", 1),), drain=True)
        p.ve("tensor_mul", tm, tm, sc0, incs=(("sV", 1),), drain=True)
        p.ve("tensor_reduce", H["rtrows"][:, 2:3], tm, AX.X, OP.add, incs=(("sV", 1),), drain=True)
        p.ve("tensor_scalar", H["rtrows"][:, 2:3], H["rtrows"][:, 2:3], 1e-9, None,
             OP.add, incs=(("sV", 1),), drain=True)
        p.ve("reciprocal", H["rtrows"][:, 3:4], H["rtrows"][:, 2:3], incs=(("sV", 1),), drain=True)
        p.ve("tensor_scalar", H["cmb8"][:, :], tm, H["rtrows"][:, 3:4], None, OP.mult,
             incs=(("sV", 1),), drain=True)
        rctx[kt] = p.n["sV"]

    def router_fin(kt):
        tw = [("sV", rctx.pop(kt))]
        if first_ps[0]:
            tw.append(("sS", p.P3["sS"]))
            first_ps[0] = False
        p.tr(H["psS"][0:8, 0:128], H["cmb8"][:, :], waits=tw, incs=(("sT", 1),))
        p.ve("tensor_copy", H["combsb"][0:8, kt * 128:(kt + 1) * 128],
             H["psS"][0:8, 0:128], waits=[("sT", p.n["sT"])], incs=(("sV", 1),))

    silu_rec = [None, None]
    first_h = True
    for c in range(HS // 128):
        if c < 8:
            router_group(c)
        w1t = p.wload(H["sw1T"][c], D)
        w2t = p.wload(H["sw2T"][c], D)
        for s2 in range(2):
            regg = p.ps_alloc(1)
            for kc in range(DC):
                w = []
                if c == 0:
                    w = [("sV", p.X2DONE[(kc, s2)])]
                if kc == 0:
                    w = regg["waits"] + w + [w1t["ready"]]
                elif kc == DC - 1 and s2 == 1:
                    w = w + [w1t["ready"]]
                p.mm(regg["ap"][:, 0:512], w1t["ap"][:, kc * 128:(kc + 1) * 128],
                     H["x2bf"][:, kc * TOK + s2 * 512: kc * TOK + s2 * 512 + 512],
                     start=(kc == 0), stop=(kc == DC - 1), waits=w,
                     incs=(("sT", 1),) if kc == DC - 1 else ())
            GT = p.n["sT"]
            if s2 == 1:
                p.wfree(w1t, GT)
            regu = p.ps_alloc(1)
            for kc in range(DC):
                p.mm(regu["ap"][:, 0:512], w2t["ap"][:, kc * 128:(kc + 1) * 128],
                     H["x2bf"][:, kc * TOK + s2 * 512: kc * TOK + s2 * 512 + 512],
                     start=(kc == 0), stop=(kc == DC - 1),
                     waits=(regu["waits"] + [w2t["ready"]]) if kc == 0
                     else ([w2t["ready"]] if (kc == DC - 1 and s2 == 1) else ()),
                     incs=(("sT", 1),) if kc == DC - 1 else ())
            if s2 == 1:
                p.wfree(w2t, p.n["sT"])
            ssl = H["silub"][:, s2 * 512:(s2 + 1) * 512]
            sw = [("sT", GT)] + ([silu_rec[s2]] if silu_rec[s2] else [])
            if first_h:
                sw += GATE
            p.sc("activation", ssl, regg["ap"][:, 0:512], AF.Silu,
                 waits=sw, incs=(("sS", 1),))
            p.ps_free(regg, "sS")
            hw = [("sS", p.n["sS"]), ("sT", p.n["sT"])]
            if first_h:
                hw += [("sV", p.P3["sV"]), ("sDbg", p.n["sDbg"])]
                first_h = False
            p.ve("tensor_mul", hsl(H, c, s2 * 512, 512), ssl, regu["ap"][:, 0:512],
                 waits=hw, incs=(("sV", 1),))
            silu_rec[s2] = ("sV", p.n["sV"])
            p.ps_free(regu, "sV")
        if c < 8:
            router_fin(c)
    HDONE = p.n["sV"]
    p.tap("d_comb", H["combsb"][:, :], [("sV", p.n["sV"])])
    p.tap("d_hbuf", H["hbufA"][:, :], [("sV", p.n["sV"])])

    so_gen = [0, 0]
    for b in range(DC):
        wta = p.wload(H["sw3T"][b][:, 0:4096], 4096)
        wtb = p.wload(H["sw3T"][b][:, 4096:8192], 4096)
        for s2 in range(2):
            reg = p.ps_alloc(1)
            for hc in range(64):
                wt = wta if hc < 32 else wtb
                w = []
                if hc == 0:
                    w = reg["waits"] + [wta["ready"], ("sV", HDONE)]
                elif hc == 32:
                    w = [wtb["ready"]]
                if s2 == 1 and hc == 31:
                    w = w + [wta["ready"]]
                if s2 == 1 and hc == 63:
                    w = w + [wtb["ready"]]
                p.mm(reg["ap"][:, 0:512],
                     wt["ap"][:, (hc % 32) * 128:((hc % 32) + 1) * 128],
                     hsl(H, hc, s2 * 512, 512),
                     start=(hc == 0), stop=(hc == 63), waits=w,
                     incs=(("sT", 1),) if hc in (31, 63) else ())
            Y3STOP = p.n["sT"]
            if s2 == 1:
                p.wfree(wta, Y3STOP - 1)
                p.wfree(wtb, Y3STOP)
            slot = s2
            sw = [("sT", p.n["sT"]), (f"sSp{slot}", 16 * so_gen[slot])]
            p.ve("tensor_copy", H["sptmpSO"][:, slot * 512:(slot + 1) * 512],
                 reg["ap"][:, 0:512], waits=sw, incs=(("sV", 1),))
            p.ps_free(reg, "sV")
            p.dma("sy", H["spso"][b * 128:(b + 1) * 128, s2 * 512:(s2 + 1) * 512],
                  H["sptmpSO"][:, slot * 512:(slot + 1) * 512], f"sSp{slot}",
                  waits=[("sV", p.n["sV"])])
            so_gen[slot] += 1
    p.P4 = dict(sV=p.n["sV"], sT=p.n["sT"], sS=p.n["sS"],
                sSp0=p.n["sSp0"], sSp1=p.n["sSp1"])


def phase5(p: Plan):
    H = p.H
    Y3S_T = p.P4["sT"]
    silu_rec = [None, None]
    t_rec = [None, None]
    first_h = True
    for e in range(NE):
        cregs = []
        for s2 in range(2):
            cr = p.ps_alloc(1)
            p.mm(cr["ap"][:, 0:512], H["esel"][0:8, e * 128:(e + 1) * 128],
                 H["combsb"][0:8, s2 * 512:(s2 + 1) * 512], True, True,
                 waits=cr["waits"] + ([("sV", p.P4["sV"])] if e == 0 and s2 == 0 else []),
                 incs=(("sT", 1),))
            cregs.append(cr)
        for hc in range(8):
            w1t = p.wload(H["w12T"][e][hc], D)
            w2t = p.wload(H["w12T"][e][hc + 8], D)
            for s2 in range(2):
                reg1 = p.ps_alloc(1)
                for kc in range(DC):
                    p.mm(reg1["ap"][:, 0:512], w1t["ap"][:, kc * 128:(kc + 1) * 128],
                         H["x2bf"][:, kc * TOK + s2 * 512: kc * TOK + s2 * 512 + 512],
                         start=(kc == 0), stop=(kc == DC - 1),
                         waits=(reg1["waits"] + [w1t["ready"]]) if kc == 0
                         else ([w1t["ready"]] if (kc == DC - 1 and s2 == 1) else ()),
                         incs=(("sT", 1),) if kc == DC - 1 else ())
                H1T = p.n["sT"]
                if s2 == 1:
                    p.wfree(w1t, H1T)
                reg2 = p.ps_alloc(1)
                for kc in range(DC):
                    p.mm(reg2["ap"][:, 0:512], w2t["ap"][:, kc * 128:(kc + 1) * 128],
                         H["x2bf"][:, kc * TOK + s2 * 512: kc * TOK + s2 * 512 + 512],
                         start=(kc == 0), stop=(kc == DC - 1),
                         waits=(reg2["waits"] + [w2t["ready"]]) if kc == 0
                         else ([w2t["ready"]] if (kc == DC - 1 and s2 == 1) else ()),
                         incs=(("sT", 1),) if kc == DC - 1 else ())
                if s2 == 1:
                    p.wfree(w2t, p.n["sT"])
                ssl = H["silub"][:, s2 * 512:(s2 + 1) * 512]
                sw = [("sT", H1T)] + ([silu_rec[s2]] if silu_rec[s2] else [])
                p.sc("activation", ssl, reg1["ap"][:, 0:512], AF.Silu,
                     waits=sw, incs=(("sS", 1),))
                p.ps_free(reg1, "sS")
                tsl = H["rltmpSO"][:, s2 * 512:(s2 + 1) * 512]
                tw = [("sS", p.n["sS"]), ("sT", p.n["sT"])] +                      ([t_rec[s2]] if t_rec[s2] else [])
                p.ve("tensor_mul", tsl, ssl, reg2["ap"][:, 0:512],
                     waits=tw, incs=(("sV", 1),))
                silu_rec[s2] = ("sV", p.n["sV"])
                p.ps_free(reg2, "sV")
                hw = [("sV", p.n["sV"])]
                if first_h:
                    hw += [("sT", Y3S_T), ("sDbg", p.n["sDbg"])]
                    first_h = False
                p.ve("tensor_mul", hsl(H, e * 8 + hc, s2 * 512, 512), tsl,
                     cregs[s2]["ap"][:, 0:512], waits=hw, incs=(("sV", 1),))
                t_rec[s2] = ("sV", p.n["sV"])
        for cr in cregs:
            p.ps_free(cr, "sV")
    HRDONE = p.n["sV"]

    ssum = [p.ps_alloc(1) for _ in range(2)]
    ssq = [p.ps_alloc(1) for _ in range(2)]
    rl_gen = [0, 0]
    rl_rec = [None, None]
    sq_rec = [None, None]
    x3_rec = [None, None]
    pend = []

    def emit_stats(b2, s2b, x2sl, sqsl, sv):
        p.mm(ssum[s2b]["ap"][0:1, 0:512], H["ones_col"][:, :], x2sl,
             start=(b2 == 0), stop=(b2 == DC - 1),
             waits=(ssum[s2b]["waits"] if b2 == 0 else []) + [("sS", sv)],
             incs=(("sT", 1),) if b2 == DC - 1 else ())
        p.mm(ssq[s2b]["ap"][0:1, 0:512], H["ones_col"][:, :], sqsl,
             start=(b2 == 0), stop=(b2 == DC - 1),
             waits=(ssq[s2b]["waits"] if b2 == 0 else []),
             incs=(("sT", 1),) if b2 == DC - 1 else ())
        x3_rec[s2b] = ("sT", p.n["sT"])
        sq_rec[s2b] = ("sT", p.n["sT"])

    for b in range(DC):
        wta = p.wload(H["w3T"][b][0:4].rearrange("e p h -> p e h"), 4096, nshape=(4, HR))
        wtb = p.wload(H["w3T"][b][4:8].rearrange("e p h -> p e h"), 4096, nshape=(4, HR))
        for s2 in range(2):
            slot = s2
            rw = [("sSp0", p.P4["sSp0"]), ("sSp1", p.P4["sSp1"])] \
                if (b == 0 and s2 == 0) else []
            if rl_rec[slot]:
                rw = rw + [rl_rec[slot]]
            if b == 0:
                rw = rw + [("sV", HRDONE)]
            p.dma("gp", H["rltmpSO"][:, slot * 512:(slot + 1) * 512],
                  H["spso"][b * 128:(b + 1) * 128, s2 * 512:(s2 + 1) * 512],
                  f"sR{slot}", waits=rw)
            rl_gen[slot] += 1
            RLRDY = (f"sR{slot}", 16 * rl_gen[slot])
            reg = p.ps_alloc(1)
            nmm = 0
            for e in range(NE):
                wt = wta if e < 4 else wtb
                for hc in range(8):
                    w = []
                    if nmm == 0:
                        w = reg["waits"] + [wta["ready"], ("sV", HRDONE)]
                    elif e == 4 and hc == 0:
                        w = [wtb["ready"]]
                    if s2 == 1 and nmm == 31:
                        w = w + [wta["ready"]]
                    if s2 == 1 and nmm == 63:
                        w = w + [wtb["ready"]]
                    p.mm(reg["ap"][:, 0:512],
                         wt["ap"][:, (e % 4) * 1024 + hc * 128:
                                  (e % 4) * 1024 + hc * 128 + 128],
                         hsl(H, e * 8 + hc, s2 * 512, 512),
                         start=(nmm == 0), stop=(nmm == 63), waits=w,
                         incs=(("sT", 1),) if nmm in (31, 63) else ())
                    nmm += 1
            Y3STOP = p.n["sT"]
            if s2 == 1:
                p.wfree(wta, Y3STOP - 1)
                p.wfree(wtb, Y3STOP)
            if pend:
                emit_stats(*pend.pop(0))
            spw = [("sSp0", p.P4["sSp0"]), ("sSp1", p.P4["sSp1"])] \
                if (b == 0 and s2 == 0) else []
            tsl = H["sqtmp5"][:, s2 * 512:(s2 + 1) * 512]
            p.ve("tensor_add", tsl, reg["ap"][:, 0:512],
                 H["rltmpSO"][:, slot * 512:(slot + 1) * 512],
                 waits=[("sT", Y3STOP), RLRDY] + spw, incs=(("sV", 1),),
                 drain=True)
            rl_rec[slot] = ("sV", p.n["sV"])
            p.ps_free(reg, "sV")
            x2sl = H["x2bf"][:, b * TOK + s2 * 512: b * TOK + s2 * 512 + 512]
            p.ve("tensor_add", x2sl, tsl, x2sl, incs=(("sV", 1),), drain=True)
            X3V = p.n["sV"]
            sqsl = H["sqtmp5"][:, (2 + s2) * 512:(3 + s2) * 512]
            qw = [("sV", X3V)] + ([sq_rec[s2]] if sq_rec[s2] else [])
            p.sc("activation", sqsl, x2sl, AF.Square, waits=qw, incs=(("sS", 1),))
            pend.append((b, s2, x2sl, sqsl, p.n["sS"]))
    while pend:
        emit_stats(*pend.pop(0))
    STATS_T = p.n["sT"]
    p.tap("d_x3bf", H["x2bf"][:, :], [("sV", p.n["sV"])])

    bregs = []
    A_ = H["srow"][0:1, 0:512]
    B_ = H["srow"][0:1, 512:1024]
    C_ = H["srow"][0:1, 1024:1536]
    for s2 in range(2):
        spw = [("sSp0", p.P4["sSp0"]), ("sSp1", p.P4["sSp1"])] if s2 == 0 else []
        p.ve("tensor_scalar", A_, ssum[s2]["ap"][0:1, 0:512], 1.0 / D, None,
             OP.mult, waits=[("sT", STATS_T)] + spw, incs=(("sV", 1),), drain=True)
        p.ve("tensor_mul", B_, A_, A_, incs=(("sV", 1),), drain=True)
        p.ve("scalar_tensor_tensor", B_, ssq[s2]["ap"][0:1, 0:512], 1.0 / D, B_,
             OP.mult, OP.subtract, incs=(("sV", 1),), drain=True)
        p.sc("activation", C_, B_, AF.Sqrt, bias=H["epsv"][0:1, 1:2],
             waits=[("sV", p.n["sV"])], incs=(("sS", 1),), drain=True)
        p.ve("reciprocal", B_, C_, waits=[("sS", p.n["sS"])], incs=(("sV", 1),),
             drain=True)
        p.ve("tensor_copy", H["lnrowB"][0:1, s2 * 512:(s2 + 1) * 512], B_,
             incs=(("sV", 1),), drain=True)
        p.ve("tensor_mul", C_, A_, B_, incs=(("sV", 1),), drain=True)
        p.ve("tensor_copy", H["lnrowB"][0:1, 1024 + s2 * 512: 1024 + (s2 + 1) * 512],
             C_, incs=(("sV", 1),), drain=True)
        p.ps_free(ssum[s2], "sV")
        p.ps_free(ssq[s2], "sV")
    br = p.ps_alloc(2)
    bm = p.ps_alloc(2)
    for s2 in range(2):
        p.mm(br["ap"][:, s2 * 512:(s2 + 1) * 512], H["ones_row"][0:1, 0:128],
             H["lnrowB"][0:1, s2 * 512:(s2 + 1) * 512], True, True,
             waits=(br["waits"] + [("sV", p.n["sV"])]) if s2 == 0 else (),
             incs=(("sT", 1),))
        p.mm(bm["ap"][:, s2 * 512:(s2 + 1) * 512], H["ones_row"][0:1, 0:128],
             H["lnrowB"][0:1, 1024 + s2 * 512: 1024 + (s2 + 1) * 512], True, True,
             waits=bm["waits"] if s2 == 0 else (), incs=(("sT", 1),))
    bregs = [(br, bm)]
    BC_T = p.n["sT"]

    u_rec = None
    out_gen = [0, 0]
    for b in range(DC):
        x2sl = H["x2bf"][:, b * TOK: (b + 1) * TOK]
        usl = H["sqtmp5"][:, 0:1024]
        w1 = [("sT", BC_T)] + ([u_rec] if u_rec else [])
        p.ve("tensor_mul", usl, x2sl, br["ap"][:, 0:1024], waits=w1,
             incs=(("sV", 1),), drain=True)
        slot = b % 2
        dst = H["outtmp"][:, 0:1024] if slot == 0 else H["outtmp2"][:, 0:1024]
        ow = [(f"sO{slot}", 16 * out_gen[slot])] if out_gen[slot] else []
        p.ve("tensor_sub", dst, usl, bm["ap"][:, 0:1024], waits=ow,
             incs=(("sV", 1),))
        u_rec = ("sV", p.n["sV"])
        p.dma("sy", H["out"][b * 128:(b + 1) * 128, 0:TOK], dst,
              f"sO{slot}", waits=[("sV", p.n["sV"])])
        out_gen[slot] += 1
    for br, bm in bregs:
        p.ps_free(br, "sV")
        p.ps_free(bm, "sV")


def build_nc(dbg=()):
    nc = bass.Bass("TRN2", target_bir_lowering=False)
    es = contextlib.ExitStack()
    H = declare(nc, dbg=dbg)
    sems = {name: es.enter_context(nc.semaphore(name)) for name in SEM_NAMES}
    plans = []
    with nc.Block() as block:
        def run(tag):
            pl = Plan(nc, tag, H, sems)
            with nc.allow_low_precision("bf16 matmul-operand pipeline by design"):
                walk(pl)
            plans.append(pl)
        block.gpsimd(lambda g: run("gp"))
        block.sync(lambda s: run("sy"))
        block.tensor(lambda t: run("te"))
        block.vector(lambda v: run("ve"))
        block.scalar(lambda s: run("sc"))
    for pl in plans[1:]:
        assert pl.n == plans[0].n, "walk counters diverged between engines"
    return nc


def host_tile_w(W):
    Out, In = W.shape
    Ob, Kc = Out // 128, In // 128
    return np.ascontiguousarray(
        W.reshape(Ob, 128, Kc, 128).transpose(0, 3, 2, 1).reshape(Ob, 128, In),
        dtype=np.float32)


def make_inmaps(inp):
    x = np.asarray(inp["L0_attn_x"], dtype=np.float32)[0]
    wq = np.asarray(inp["L0_attn_wq"], np.float32)
    wk = np.asarray(inp["L0_attn_wk"], np.float32)
    wv = np.asarray(inp["L0_attn_wv"], np.float32)
    wo = np.asarray(inp["L0_attn_wo"], np.float32)
    qn = np.asarray(inp["L0_attn_q_norm_w"], np.float32)
    kn = np.asarray(inp["L0_attn_k_norm_w"], np.float32)
    w12 = np.asarray(inp["L0_ffn_w12"], np.float32)
    w3 = np.asarray(inp["L0_ffn_w3"], np.float32)
    rw = np.asarray(inp["L0_ffn_router_w"], np.float32)
    eb = np.asarray(inp["L0_ffn_expert_bias"], np.float32)
    sw1 = np.asarray(inp["L0_ffn_sw1"], np.float32)
    sw2 = np.asarray(inp["L0_ffn_sw2"], np.float32)
    sw3 = np.asarray(inp["L0_ffn_sw3"], np.float32)

    com = {}
    com["wqT"] = host_tile_w(wq)
    com["wkT"] = host_tile_w(wk)
    com["wvT"] = np.ascontiguousarray(wv.T)
    com["woT"] = host_tile_w(wo)
    com["rwT"] = np.ascontiguousarray(rw.T)
    com["sw1T"] = host_tile_w(sw1)
    com["sw2T"] = host_tile_w(sw2)
    com["sw3T"] = host_tile_w(sw3)
    com["w12T"] = np.stack([host_tile_w(w12[e]) for e in range(NE)])
    w3t = np.stack([host_tile_w(w3[e]) for e in range(NE)])
    com["w3T"] = np.ascontiguousarray(w3t.transpose(1, 0, 2, 3))
    com["qnw"] = qn.reshape(128, 1)
    com["knw"] = kn.reshape(128, 1)
    com["ebias"] = np.ascontiguousarray(np.broadcast_to(eb[None, :], (128, NE)))

    import jax
    cpu = jax.devices("cpu")[0]
    with jax.default_device(cpu):
        import jax.numpy as jnp
        _inv = 1.0 / 10000.0 ** (jnp.arange(0, HD, 2, dtype=jnp.float32) / HD)
        _pos = jnp.arange(0, S, dtype=jnp.int32)
        _ang = _pos[:, None].astype(jnp.float32) * _inv[None, :]
        _cosg = np.asarray(jnp.cos(_ang), np.float32)
        _sing = np.asarray(jnp.sin(_ang), np.float32)
    maps = []
    for c in range(8):
        m = dict(com)
        own = x[c * TOK:(c + 1) * TOK]
        halo = x[c * TOK - TOK: c * TOK] if c > 0 else np.zeros((TOK, D), np.float32)
        m["xT"] = np.ascontiguousarray(np.concatenate([halo, own], 0).T)
        lo = c * TOK - TOK
        csl = np.zeros((TT, HD // 2), np.float32)
        ssl = np.zeros((TT, HD // 2), np.float32)
        n0 = max(0, -lo)
        csl[n0:] = _cosg[max(lo, 0):lo + TT]
        ssl[n0:] = _sing[max(lo, 0):lo + TT]
        csl[:n0] = _cosg[0]
        ssl[:n0] = _sing[0]
        m["tabCin"] = np.ascontiguousarray(np.concatenate([csl.T, csl.T], 0))
        m["tabSin"] = np.ascontiguousarray(np.concatenate([-ssl.T, ssl.T], 0))
        kbc = np.zeros((128, TT // 128), np.float32)
        if c == 0:
            kbc[:, :TOK // 128] = NEG
        m["kbc"] = kbc
        maps.append(m)
    return maps


from concourse.bass_utils import run_bass_kernel_spmd


def kernel(**inputs):
    maps = make_inmaps(inputs)
    nc = build_nc()
    res = run_bass_kernel_spmd(nc, maps, core_ids=list(range(8)))
    parts = [np.ascontiguousarray(np.asarray(res.results[c]["out"]).T)
             for c in range(8)]
    out = np.concatenate(parts, axis=0).reshape(1, S, D).astype(np.float32)
    return out


# version 26 (final)
# speedup vs baseline: 1.1114x; 1.1114x over previous
import numpy as np
import contextlib
import concourse.bass as bass
import concourse.mybir as mybir

F32 = mybir.dt.float32
BF16 = mybir.dt.bfloat16
I32 = mybir.dt.int32
AF = mybir.ActivationFunctionType
OP = mybir.AluOpType
AX = mybir.AxisListType

D, S = 2048, 8192
TOK, TT = 1024, 2048
NH, NKV, HD = 16, 4, 128
WIN = 1024
HS, HR, NE = 8192, 1024, 8
SCALE = float(HD) ** -0.5
DC = D // 128
NEG = -1e9
KBy = 1024

SB_BASE = 16512
R_A, R_B, R_C, R_D = SB_BASE, SB_BASE + 65536, SB_BASE + 98304, SB_BASE + 114688
R_E, R_W, R_T, R_S = SB_BASE + 131328, SB_BASE + 164096, SB_BASE + 188672, SB_BASE + 205056
W_SLOTS, W_ELEMS = 3, 4096

SEM_NAMES = (["sT", "sV", "sS", "sG", "sXB", "sConst", "sXF", "sSg0", "sSg1", "sXB0", "sXB1", "sXB2", "sXB3",
              "sSp0", "sSp1", "sR0", "sR1", "sO0", "sO1", "sDbg"]
             + [f"sW{i}" for i in range(W_SLOTS)])


def declare(nc, dbg=()):
    H = {}
    dt = nc.dram_tensor
    H["xT"] = dt("xT", [D, TT], F32, kind="ExternalInput").ap()
    H["tabCin"] = dt("tabCin", [128, TT], F32, kind="ExternalInput").ap()
    H["tabSin"] = dt("tabSin", [128, TT], F32, kind="ExternalInput").ap()
    H["kbc"] = dt("kbc", [128, TT // 128], F32, kind="ExternalInput").ap()
    H["qnw"] = dt("qnw", [128, 1], F32, kind="ExternalInput").ap()
    H["knw"] = dt("knw", [128, 1], F32, kind="ExternalInput").ap()
    H["ebias"] = dt("ebias", [128, NE], F32, kind="ExternalInput").ap()
    H["wqT"] = dt("wqT", [NH, 128, D], F32, kind="ExternalInput").ap()
    H["wkT"] = dt("wkT", [NKV, 128, D], F32, kind="ExternalInput").ap()
    H["wvT"] = dt("wvT", [D, NKV * HD], F32, kind="ExternalInput").ap()
    H["woT"] = dt("woT", [DC, 128, D], F32, kind="ExternalInput").ap()
    H["rwT"] = dt("rwT", [D, NE], F32, kind="ExternalInput").ap()
    H["sw1T"] = dt("sw1T", [HS // 128, 128, D], F32, kind="ExternalInput").ap()
    H["sw2T"] = dt("sw2T", [HS // 128, 128, D], F32, kind="ExternalInput").ap()
    H["sw3T"] = dt("sw3T", [DC, 128, HS], F32, kind="ExternalInput").ap()
    H["w12T"] = dt("w12T", [NE, 16, 128, D], F32, kind="ExternalInput").ap()
    H["w3T"] = dt("w3T", [DC, NE, 128, HR], F32, kind="ExternalInput").ap()
    H["out"] = dt("out", [D, TOK], F32, kind="ExternalOutput").ap()
    H["spso"] = dt("spso", [D, TOK], F32, kind="Internal").ap()

    A = nc.alloc_sbuf_tensor_at
    nc.alloc_sbuf_tensor("arena", [128, 52384], F32)
    H["xbf"] = A("xbf", [128, DC * TT], BF16, offset=R_A)
    H["xf32"] = A("xf32", [128, DC * TOK], F32, offset=R_A)
    H["hbufA"] = A("hbufA", [128, 32 * TOK], BF16, offset=R_A)
    H["scr1"] = A("scr1", [128, TT], F32, offset=R_B)
    H["scr2"] = A("scr2", [128, TT], F32, offset=R_B + 8 * KBy)
    H["scrI"] = A("scrI", [128, TT], I32, offset=R_B + 16 * KBy)
    H["scr3"] = A("scr3", [128, TT], F32, offset=R_B + 24 * KBy)
    H["qhat"] = A("qhat", [128, NH * TOK], BF16, offset=R_B)
    H["hbufB1"] = A("hbufB1", [128, 16 * TOK], BF16, offset=R_B)
    H["khat"] = A("khat", [128, NKV * TT], BF16, offset=R_C)
    H["hbufB2"] = A("hbufB2", [128, 8 * TOK], BF16, offset=R_C)
    H["vtm"] = A("vtm", [128, NKV * 16 * 130], BF16, offset=R_D)
    H["hbufB3"] = A("hbufB3", [128, 8 * TOK], BF16, offset=R_D)
    H["wvsb"] = A("wvsb", [128, DC * (NKV * HD)], BF16, offset=R_E)
    H["rsb"] = A("rsb", [128, 2 * 512], F32, offset=R_E + 16 * KBy)
    H["ropet"] = A("ropet", [128, 512], BF16, offset=R_E + 16 * KBy)
    H["sqb1"] = A("sqb1", [128, 2 * 512], BF16, offset=R_E + 20 * KBy)
    H["sgb2"] = A("sgb2", [128, 2 * 512], BF16, offset=R_E + 22 * KBy)
    H["srow1"] = A("srow1", [1, 512], F32, offset=R_E + 24 * KBy)
    H["rstdb1"] = A("rstdb1", [1, 2 * 512], BF16, offset=R_E + 26 * KBy)
    H["qbf1"] = A("qbf1", [128, 3 * 512], BF16, offset=R_E + 28 * KBy)
    H["pring"] = A("pring", [128, 12 * 512], BF16, offset=R_E)
    H["otmp"] = A("otmp", [128, 2 * 128], BF16, offset=R_E + 12 * KBy)
    H["mrow"] = A("mrow", [1, 1024], BF16, offset=R_E + 12 * KBy + 512)
    H["mcol"] = A("mcol", [128, 4], F32, offset=R_E + 15 * KBy)
    H["mcolb"] = A("mcolb", [128, 2], BF16, offset=R_E + 15 * KBy + 256)
    H["x2bf"] = A("x2bf", [128, DC * TOK], BF16, offset=R_E)
    H["x3bf"] = A("x3bf", [128, DC * TOK], BF16, offset=R_E)
    H["wring"] = A("wring", [128, W_SLOTS * W_ELEMS], BF16, offset=R_W)
    H["tabC"] = A("tabC", [128, TT], F32, offset=R_T)
    H["tabS"] = A("tabS", [128, TT], F32, offset=R_T + 8 * KBy)
    H["x1bf"] = A("x1bf", [128, 2 * 512], BF16, offset=R_T)
    H["sqb3"] = A("sqb3", [128, 2 * 512], BF16, offset=R_T + 2 * KBy)
    H["silub"] = A("silub", [128, 2 * 512], BF16, offset=R_T)
    H["sptmpSO"] = A("sptmpSO", [128, 2 * 512], F32, offset=R_T + 2 * KBy)
    H["sqtmp5"] = A("sqtmp5", [128, 4 * 512], BF16, offset=R_T)
    H["rltmpSO"] = A("rltmpSO", [128, 2 * 512], BF16, offset=R_T + 14 * KBy)
    H["lnrowB"] = A("lnrowB", [1, 2048], BF16, offset=R_T + 10 * KBy)
    H["outtmp2"] = A("outtmp2", [128, 1024], F32, offset=R_T + 2 * KBy)
    H["srow"] = A("srow", [1, 3 * 512], F32, offset=R_T + 4 * KBy)
    off = R_S

    def sm(name, shape, dtp):
        nonlocal off
        t = A(name, shape, dtp, offset=off)
        elems = int(np.prod(shape[1:]))
        sz = elems * (2 if dtp == BF16 else 4)
        off += (sz + 31) // 32 * 32
        return t

    H["ident"] = sm("ident", [128, 128], BF16)
    H["perm"] = sm("perm", [128, 128], BF16)
    H["ones_col"] = sm("ones_col", [128, 1], BF16)
    H["ones_row"] = sm("ones_row", [1, 128], BF16)
    H["mask_c_qm"] = sm("mask_c_qm", [128, 128], BF16)
    H["mask_e_qm"] = sm("mask_e_qm", [128, 128], BF16)
    H["mask_c_km"] = sm("mask_c_km", [128, 128], BF16)
    H["mask_e_km"] = sm("mask_e_km", [128, 128], BF16)
    H["combsb"] = sm("combsb", [8, TOK], BF16)
    H["selsb"] = sm("selsb", [128, 4 * NE], F32)
    H["rtrows"] = sm("rtrows", [128, 8], F32)
    H["rwsb"] = sm("rwsb", [128, DC * NE], BF16)
    H["qnwsb"] = sm("qnwsb", [128, 1], F32)
    H["knwsb"] = sm("knwsb", [128, 1], F32)
    H["ebsb"] = sm("ebsb", [128, NE], F32)
    H["cols"] = sm("cols", [128, 8], F32)
    H["v11"] = sm("v11", [1, 8], F32)
    H["epsv"] = sm("epsv", [1, 4], F32)
    H["kbcsb"] = sm("kbcsb", [128, TT // 128], F32)
    H["cmb8"] = sm("cmb8", [128, NE], BF16)
    H["sgb"] = sm("sgb", [128, 512], BF16)
    H["esel"] = sm("esel", [8, NE * 128], BF16)
    assert off <= SB_BASE + 212863, f"smalls overflow {off}"
    H["outtmp"] = A("outtmp", [128, 2 * 512], F32, offset=R_S)


    H["psA"] = nc.alloc_psum_tensor("psA", [128, 7 * 512], F32)
    H["psS"] = nc.alloc_psum_tensor("psS", [128, 1024], BF16)

    H["_dbg"] = {}
    for name, shape, dtp in [
        ("d_tabC", [128, TT], F32), ("d_tabS", [128, TT], F32),
        ("d_khat", [128, NKV * TT], BF16), ("d_qhat", [128, NH * TOK], BF16),
        ("d_vtm", [128, NKV * 16 * 130], BF16),
        ("d_attnout", [128, NH * TOK], BF16),
        ("d_x2bf", [128, DC * TOK], BF16),
        ("d_comb", [8, TOK], BF16),
        ("d_hbuf", [128, 32 * TOK], BF16),
        ("d_so", [D, TOK], F32),
        ("d_x3bf", [128, DC * TOK], BF16),
        ("d_cols", [128, 8], F32),
        ("d_ang", [128, TT], F32),
        ("d_kf", [128, TT], F32),
        ("d_r", [128, TT], F32),
    ]:
        if name in dbg:
            H["_dbg"][name] = dt(name, shape, dtp, kind="ExternalOutput").ap()
    return H


class Plan:
    def __init__(self, nc, ee, H, sems):
        self.nc, self.ee, self.H, self.sems = nc, ee, H, sems
        self.n = {k: 0 for k in SEM_NAMES}
        self.TE, self.VE, self.SC = nc.tensor, nc.vector, nc.scalar
        self.GP, self.SY = nc.gpsimd, nc.sync
        self.bank_rec = [None] * 7
        self.bank_busy = [False] * 7
        self.bank_next = 0
        self.wslot_gen = [0] * W_SLOTS
        self.wslot_rec = [None] * W_SLOTS
        self.wslot_next = 0

    def _eng(self, tag):
        return {"te": self.TE, "ve": self.VE, "sc": self.SC,
                "gp": self.GP, "sy": self.SY}[tag]

    def _waits(self, tag, waits):
        if self.ee == tag:
            for (s, v) in waits:
                if v > 0:
                    self._eng(tag).wait_ge(self.sems[s], v)

    def _fin(self, tag, inst, incs):
        for (s, k) in incs:
            self.n[s] += k
            if inst is not None:
                inst.then_inc(self.sems[s], k)

    def mm(self, out, lhsT, rhs, start, stop, waits=(), incs=()):
        self._waits("te", waits)
        inst = self.TE.matmul(out, lhsT, rhs, start=start, stop=stop) \
            if self.ee == "te" else None
        self._fin("te", inst, incs)

    def tr(self, out, in_, waits=(), incs=()):
        self._waits("te", waits)
        inst = self.TE.transpose(out, in_, self.H["ident"][:, 0:128]) \
            if self.ee == "te" else None
        self._fin("te", inst, incs)

    def _comp(self, tag, fn, args, kwargs, waits, incs, drain):
        self._waits(tag, waits)
        inst = getattr(self._eng(tag), fn)(*args, **kwargs) if self.ee == tag else None
        self._fin(tag, inst, incs)
        if drain and self.ee == tag:
            self._eng(tag).drain()

    def ve(self, fn, *args, waits=(), incs=(), drain=False, **kw):
        self._comp("ve", fn, args, kw, waits, incs, drain)

    def sc(self, fn, *args, waits=(), incs=(), drain=False, **kw):
        self._comp("sc", fn, args, kw, waits, incs, drain)

    def gp(self, fn, *args, waits=(), incs=(), drain=True, **kw):
        self._comp("gp", fn, args, kw, waits, incs, drain)

    def dma(self, tag, out, in_, sem, waits=()):
        self._waits(tag, waits)
        inst = self._eng(tag).dma_start(out=out, in_=in_) if self.ee == tag else None
        self._fin(tag, inst, ((sem, 16),))

    def tap(self, name, src_ap, waits):
        if name in self.H["_dbg"]:
            self.dma("sy", self.H["_dbg"][name][:, :], src_ap, "sDbg", waits=waits)

    def ps_alloc(self, nbank=1):
        for _ in range(9):
            b = self.bank_next
            if b + nbank > 7:
                b = 0
            if all(not self.bank_busy[b + i] for i in range(nbank)):
                waits = []
                for i in range(nbank):
                    self.bank_busy[b + i] = True
                    if self.bank_rec[b + i]:
                        waits.append(self.bank_rec[b + i])
                self.bank_next = (b + nbank) % 7
                return {"b": b, "nb": nbank,
                        "ap": self.H["psA"][:, b * 512:(b + nbank) * 512],
                        "waits": waits}
            self.bank_next = (self.bank_next + 1) % 7
        raise RuntimeError(f"psum full busy={self.bank_busy} next={self.bank_next}")

    def ps_free(self, reg, sem):
        v = self.n[sem]
        for i in range(reg["nb"]):
            self.bank_busy[reg["b"] + i] = False
            self.bank_rec[reg["b"] + i] = (sem, v)

    def wload(self, dram_ap, nelem, nshape=None):
        s = self.wslot_next
        self.wslot_next = (s + 1) % W_SLOTS
        waits = [self.wslot_rec[s]] if self.wslot_rec[s] else []
        ap = self.H["wring"][:, s * W_ELEMS: s * W_ELEMS + nelem]
        dst = ap if nshape is None else ap.rearrange("p (a b) -> p a b", a=nshape[0])
        self.dma("gp", dst, dram_ap, f"sW{s}", waits=waits)
        self.wslot_gen[s] += 1
        return {"ap": ap, "ready": (f"sW{s}", 16 * self.wslot_gen[s]), "s": s}

    def wfree(self, h, val=None):
        self.wslot_rec[h["s"]] = ("sT", self.n["sT"] if val is None else val)


def walk(p: Plan):
    H = p.H
    setup(p)
    phase1(p)
    phase2(p)
    phase3(p)
    phase4(p)
    phase5(p)
    p._waits("sy", [("sO0", p.n["sO0"]), ("sO1", p.n["sO1"]),
                    ("sDbg", p.n["sDbg"])])


def setup(p: Plan):
    H = p.H
    p.dma("gp", H["kbcsb"][:, :], H["kbc"][:, :], "sConst")
    p.dma("gp", H["qnwsb"][:, :], H["qnw"][:, :], "sConst")
    p.dma("gp", H["knwsb"][:, :], H["knw"][:, :], "sConst")
    p.dma("gp", H["ebsb"][:, :], H["ebias"][:, :], "sConst")
    for kc in range(DC):
        p.dma("gp", H["rwsb"][:, kc * NE:(kc + 1) * NE],
              H["rwT"][kc * 128:(kc + 1) * 128, :], "sConst")
    xbf4 = H["xbf"].ap().rearrange("p (k t) -> p k t", k=DC)
    xT4 = H["xT"][:, :].rearrange("(k p) t -> p k t", p=128)
    for qtr in range(4):
        p.dma("gp", xbf4[:, :, qtr * 512:(qtr + 1) * 512],
              xT4[:, :, qtr * 512:(qtr + 1) * 512], f"sXB{qtr}")
    p.dma("gp", H["wvsb"].ap().rearrange("p (k o) -> p k o", k=DC),
          H["wvT"][:, :].rearrange("(k p) o -> p k o", p=128), "sConst")

    g = p.gp
    g("memset", H["ident"][:, :], 1.0, incs=(("sG", 1),))
    g("affine_select", H["ident"][:, :], H["ident"][:, :], [[1, 128]],
      OP.is_equal, 0.0, base=0, channel_multiplier=-1,
      waits=[("sG", p.n["sG"])], incs=(("sG", 1),))
    g("memset", H["perm"][:, :], 1.0, incs=(("sG", 1),))
    g("affine_select", H["perm"][:, :], H["perm"][:, :], [[1, 128]],
      OP.is_equal, 0.0, base=64, channel_multiplier=-1,
      waits=[("sG", p.n["sG"])], incs=(("sG", 1),))
    tmp = H["sqb1"][:, 0:128]
    g("memset", tmp, 1.0, incs=(("sG", 1),))
    g("affine_select", tmp, tmp, [[1, 128]], OP.is_equal, 0.0,
      base=-64, channel_multiplier=-1,
      waits=[("sG", p.n["sG"])], incs=(("sG", 1),))
    p.ve("tensor_add", H["perm"][:, :], H["perm"][:, :], tmp,
         waits=[("sG", p.n["sG"])], incs=(("sV", 1),))
    for name, base, cm, step in [
        ("mask_c_qm", 0, 1, -1),
        ("mask_e_qm", -1, -1, 1),
        ("mask_c_km", 0, -1, 1),
        ("mask_e_km", -1, 1, -1),
    ]:
        g("memset", H[name][:, :], 0.0, incs=(("sG", 1),))
        g("affine_select", H[name][:, :], H[name][:, :], [[step, 128]],
          OP.is_ge, NEG, base=base, channel_multiplier=cm,
          waits=[("sG", p.n["sG"])], incs=(("sG", 1),))
    g("memset", H["esel"][:, :], 1.0, incs=(("sG", 1),))
    for e in range(NE):
        g("affine_select", H["esel"][0:8, e * 128:(e + 1) * 128],
          H["esel"][0:8, e * 128:(e + 1) * 128], [[0, 128]],
          OP.is_equal, 0.0, base=-e, channel_multiplier=1,
          waits=[("sG", p.n["sG"])] if e == 0 else (), incs=(("sG", 1),))
    g("memset", H["ones_col"][:, :], 1.0, incs=(("sG", 1),))
    g("memset", H["ones_row"][:, :], 1.0, incs=(("sG", 1),))
    g("memset", H["vtm"][:, :], 1.0, incs=(("sG", 1),))
    GDONE = p.n["sG"]

    p.dma("gp", H["tabC"][:, :], H["tabCin"][:, :], "sConst")
    p.dma("gp", H["tabS"][:, :], H["tabSin"][:, :], "sConst")
    p.ve("memset", H["epsv"][0:1, 0:1], 1e-6, incs=(("sV", 1),))
    p.ve("memset", H["epsv"][0:1, 1:2], 1e-5, incs=(("sV", 1),))
    p.ve("memset", H["cols"][:, 0:1], 0.0078125, incs=(("sV", 1),))
    p.sc("activation", H["cols"][:, 1:2], H["cols"][:, 0:1], AF.Sqrt,
         waits=[("sV", p.n["sV"])], incs=(("sS", 1),))
    p.SETUP = dict(sV=p.n["sV"], sS=p.n["sS"], sG=p.n["sG"],
                   sConst=p.n["sConst"], sXB=p.n["sXB"])


def _p1_pipe(p: Plan, which, hook=None, qdone=None):
    H = p.H
    st = p.SETUP
    xw = [("sConst", st["sConst"])]
    if which == "k":
        items = [(b, s4, s4 * 512, 4) for b in range(NKV) for s4 in range(4)]
        wdram, wcol, outbuf, outTT = H["wkT"], H["knwsb"], H["khat"], TT
        xq = lambda tok0: [(f"sXB{tok0 // 512}", 16)]
    else:
        items = [(b, s2, 1024 + s2 * 512, 2) for b in range(NH) for s2 in range(2)]
        wdram, wcol, outbuf, outTT = H["wqT"], H["qnwsb"], H["qhat"], TOK
        xq = lambda tok0: [(f"sXB{tok0 // 512}", 16)]
    n = len(items)
    state = {}
    sq_rec = [None] * 2
    qw_rec = [None] * 3
    sr_rec = [None] * 2
    rb_rec = [None] * 2
    sg_rec = [None, None]
    wt = None

    def S0(i):
        nonlocal wt
        b, sl, tok0, nsl = items[i]
        if sl == 0:
            wt = p.wload(wdram[b], D)
        reg = p.ps_alloc(1)
        for kc in range(DC):
            w = ()
            if kc == 0:
                w = reg["waits"] + [wt["ready"]] + ((xw + xq(tok0)) if i < nsl else [])
            elif kc == DC - 1 and sl == nsl - 1:
                w = [wt["ready"]]
            p.mm(reg["ap"][:, 0:512], wt["ap"][:, kc * 128:(kc + 1) * 128],
                 H["xbf"][:, kc * TT + tok0: kc * TT + tok0 + 512],
                 start=(kc == 0), stop=(kc == DC - 1), waits=w,
                 incs=(("sT", 1),) if kc == DC - 1 else ())
        if sl == nsl - 1:
            p.wfree(wt, p.n["sT"])
        PROJ = p.n["sT"]
        sq = H["sqb1"][:, (i % 2) * 512:(i % 2) * 512 + 512]
        w1 = [("sT", PROJ)] + ([sq_rec[i % 2]] if sq_rec[i % 2] else [])
        p.sc("activation", sq, reg["ap"][:, 0:512], AF.Square, waits=w1,
             incs=(("sS", 1),))
        qw = H["qbf1"][:, (i % 3) * 512:(i % 3) * 512 + 512]
        w2 = ([qw_rec[i % 3]] if qw_rec[i % 3] else [])
        p.sc("activation", qw, reg["ap"][:, 0:512], AF.Copy, scale=wcol[:, 0:1],
             waits=w2, incs=(("sS", 1),))
        QW = p.n["sS"]
        p.ps_free(reg, "sS")
        sg0 = (i % 2) * 512
        sem = f"sSg{i % 2}"
        dw = [("sS", QW)] + ([sg_rec[i % 2]] if sg_rec[i % 2] else [])
        p.dma("sy", H["sgb2"][0:64, sg0:sg0 + 512],
              qw[64:128, :], sem, waits=dw)
        p.dma("sy", H["sgb2"][64:128, sg0:sg0 + 512],
              qw[0:64, :], sem)
        state[i] = {"qw": qw, "sq": sq, "sqv": p.n["sS"] - 1, "tok0": tok0,
                    "b": b, "sg": (sem, p.n[sem]), "sgslot": sg0}

    def S1(i):
        stt = state[i]
        ttok = stt["tok0"]
        sg = H["sgb2"][:, stt["sgslot"]:stt["sgslot"] + 512]
        t1 = H["ropet"][:, 0:512]
        p.ve("tensor_mul", t1, stt["qw"], H["tabC"][:, ttok:ttok + 512],
             waits=[("sConst", p.SETUP["sConst"]), ("sS", stt["sqv"] + 1)],
             incs=(("sV", 1),))
        p.ve("tensor_mul", sg, sg, H["tabS"][:, ttok:ttok + 512],
             waits=[stt["sg"]], incs=(("sV", 1),), drain=True)
        p.ve("tensor_add", stt["qw"], t1, sg, incs=(("sV", 1),), drain=True)
        sg_rec[i % 2] = ("sV", p.n["sV"])
        stt["uv"] = p.n["sV"]
        ssreg = p.ps_alloc(1)
        p.mm(ssreg["ap"][0:1, 0:512], H["ones_col"][:, :], stt["sq"], True, True,
             waits=ssreg["waits"] + [("sS", stt["sqv"])], incs=(("sT", 1),))
        sq_rec[i % 2] = ("sT", p.n["sT"])
        sr = H["srow1"][0:1, 0:512]
        w1 = [("sT", p.n["sT"])] + ([sr_rec[0]] if sr_rec[0] else [])
        p.sc("activation", sr, ssreg["ap"][0:1, 0:512], AF.Sqrt,
             scale=1.0 / HD, bias=H["epsv"][0:1, 0:1], waits=w1, incs=(("sS", 1),))
        p.ps_free(ssreg, "sS")
        rb = H["rstdb1"][0:1, (i % 2) * 512:(i % 2) * 512 + 512]
        w2 = [("sS", p.n["sS"])] + ([rb_rec[i % 2]] if rb_rec[i % 2] else [])
        p.ve("reciprocal", rb, sr, waits=w2, incs=(("sV", 1),))
        sr_rec[0] = ("sV", p.n["sV"])
        stt["rb"] = rb
        stt["rbv"] = p.n["sV"]

    def S2(i):
        stt = state[i]
        breg = p.ps_alloc(1)
        p.mm(breg["ap"][:, 0:512], H["ones_row"][0:1, 0:128], stt["rb"], True, True,
             waits=breg["waits"] + [("sV", stt["rbv"])], incs=(("sT", 1),))
        rb_rec[i % 2] = ("sT", p.n["sT"])
        b = stt["b"]
        out0 = b * outTT + (stt["tok0"] - (1024 if which == "q" else 0))
        p.ve("tensor_mul", outbuf[:, out0:out0 + 512], stt["qw"],
             breg["ap"][:, 0:512], waits=[("sT", p.n["sT"])], incs=(("sV", 1),))
        qw_rec[i % 3] = ("sV", p.n["sV"])
        if qdone is not None and i % 2 == 1:
            qdone[i // 2] = p.n["sV"]
        p.ps_free(breg, "sV")
        del state[i]

    for step in range(n + 2):
        if step < n:
            S0(step)
        if hook is not None:
            hook(step)
        if step >= 1 and step - 1 < n:
            S1(step - 1)
        if step >= 2:
            S2(step - 2)


def phase1(p: Plan):
    H = p.H
    st = p.SETUP
    p.DUMMY = p.ps_alloc(1)
    p.mm(p.DUMMY["ap"][:, 0:128], H["ident"][:, :], H["ident"][:, :],
         start=True, stop=False,
         waits=p.DUMMY["waits"] + [("sG", st["sG"])])
    _p1_pipe(p, "k")
    p.tap("d_khat", H["khat"][:, :], [("sV", p.n["sV"])])
    for kt in range(16):
        reg = p.ps_alloc(1)
        for kc in range(DC):
            p.mm(reg["ap"][:, 0:NKV * HD],
                 H["xbf"][:, kc * TT + kt * 128: kc * TT + (kt + 1) * 128],
                 H["wvsb"][:, kc * 512:(kc + 1) * 512],
                 start=(kc == 0), stop=(kc == DC - 1),
                 waits=(reg["waits"] + [("sConst", st["sConst"]),
                        (f"sXB{kt // 4}", 16)]) if kc == 0 else (),
                 incs=(("sT", 1),) if kc == DC - 1 else ())
        out = H["vtm"].ap().rearrange("p (t h e) -> p t h e", t=16, h=4, e=130)
        p.ve("tensor_copy", out[:, kt, :, 0:128],
             reg["ap"].rearrange("p (h e) -> p h e", h=4)[:, :, 0:128],
             waits=[("sT", p.n["sT"]), ("sG", st["sG"])], incs=(("sV", 1),))
        p.ps_free(reg, "sV")
    p.tap("d_vtm", H["vtm"][:, :], [("sV", p.n["sV"])])
    p.P1 = dict(sV=p.n["sV"], sT=p.n["sT"])


def phase2(p: Plan):
    H = p.H
    st2 = {"pslot_rec": [None] * 12, "oslot_rec": [None] * 2, "tslot_rec": [None] * 2,
           "mslot_rec": [None] * 4, "pnext": 0, "mrow_rec": [None, None],
           "mcb_rec": [None, None], "rcp_rec": [None, None], "qready": 0}
    groups = [(h, qg) for h in range(NH) for qg in range(2)]
    qmout = {}
    qdone = {}
    gidx = [0]

    def tick():
        gi = gidx[0]
        if gi > len(groups):
            return
        if gi < len(groups):
            st2["qready"] = qdone[groups[gi][0]]
            qmout[gi] = _p2_qm(p, groups[gi], gi, st2)
        if gi >= 1:
            _p2_km(p, groups[gi - 1], gi - 1, st2, qmout[gi - 1])
        gidx[0] += 1

    _p1_pipe(p, "q", hook=None, qdone=qdone)
    p.tap("d_qhat", H["qhat"][:, :], [("sV", p.n["sV"])])
    while gidx[0] <= len(groups):
        tick()
    p.mm(p.DUMMY["ap"][:, 0:128], H["ident"][:, :], H["ident"][:, :],
         start=False, stop=True, incs=(("sT", 1),))
    p.ps_free(p.DUMMY, "sT")
    p.tap("d_attnout", H["qhat"][:, :], [("sV", p.n["sV"])])
    p.P2 = dict(sV=p.n["sV"], sT=p.n["sT"], sS=p.n["sS"])


def _p2_qm(p: Plan, g, gi, st2):
    H = p.H
    h, qg = g
    q0 = qg * 512
    kv = h // 4
    par = gi % 2
    regs = [None] * 4
    red_v = [None] * 4
    for t in range(5):
        if t < 4:
            qoff = q0 + t * 128
            k0 = (4 * qg + t) * 128
            lhs_q = H["qhat"][:, h * TOK + qoff: h * TOK + qoff + 128]
            reg = p.ps_alloc(3)
            regs[t] = reg
            first_w = reg["waits"] + ([("sV", st2["qready"])] if t == 0 else [])
            for pi, (pk, pw) in enumerate([(0, 512), (512, 512), (1024, 128)]):
                p.mm(reg["ap"][:, pk:pk + pw], lhs_q,
                     H["khat"][:, kv * TT + k0 + pk: kv * TT + k0 + pk + pw],
                     start=True, stop=True,
                     waits=first_w if pi == 0 else (),
                     incs=(("sT", 1),) if pi == 2 else ())
            mw = [("sT", p.n["sT"])]
            if st2["mcb_rec"][t % 2]:
                mw.append(st2["mcb_rec"][t % 2])
            p.ve("tensor_reduce", H["mcolb"][:, t % 2: t % 2 + 1],
                 reg["ap"][:, 0:1152], AX.X, OP.max, negate=True,
                 waits=mw, incs=(("sV", 1),))
            red_v[t] = p.n["sV"]
            p.ps_free(reg, "sV")
        if t >= 1:
            tp = t - 1
            tw = [("sV", red_v[tp])]
            if st2["mslot_rec"][tp]:
                tw.append(st2["mslot_rec"][tp])
            p.tr(H["psS"][0:1, tp * 128:(tp + 1) * 128],
                 H["mcolb"][:, tp % 2: tp % 2 + 1], waits=tw, incs=(("sT", 1),))
            st2["mcb_rec"][tp % 2] = ("sT", p.n["sT"])
            st2["mslot_rec"][tp] = None
    cw = [("sT", p.n["sT"])]
    if st2["mrow_rec"][par]:
        cw.append(st2["mrow_rec"][par])
    p.ve("tensor_copy", H["mrow"][0:1, par * 512:(par + 1) * 512],
         H["psS"][0:1, 0:512], waits=cw, incs=(("sV", 1),))
    for t in range(4):
        st2["mslot_rec"][t] = ("sV", p.n["sV"])
    return p.n["sV"]


def _p2_km(p: Plan, g, gi, st2, MROW_READY):
    H = p.H
    h, qg = g
    q0 = qg * 512
    kv = h // 4
    par = gi % 2
    oreg = p.ps_alloc(2)
    OOFF = [0, 132, 264, 512]
    av_done = [None] * 4
    exp_done = [None] * 12
    for jj in range(12):
        t_lo, t_hi = max(0, jj - 8), min(3, jj)
        W = (t_hi - t_lo + 1) * 128
        kb_i = 4 * qg + jj
        reg = p.ps_alloc(1)
        p.mm(reg["ap"][:, 0:W],
             H["khat"][:, kv * TT + kb_i * 128: kv * TT + (kb_i + 1) * 128],
             H["qhat"][:, h * TOK + q0 + t_lo * 128:
                       h * TOK + q0 + t_lo * 128 + W],
             start=True, stop=False,
             waits=reg["waits"] + ([("sV", MROW_READY)] if jj == 0 else []))
        if jj >= 8:
            c0 = (jj - 8 - t_lo) * 128
            p.mm(reg["ap"][:, c0:c0 + 128], H["ident"][:, :],
                 H["mask_c_km"][:, :], start=False, stop=False)
        if jj <= 3:
            c0 = (jj - t_lo) * 128
            p.mm(reg["ap"][:, c0:c0 + 128], H["ident"][:, :],
                 H["mask_e_km"][:, :], start=False, stop=False)
        p.mm(reg["ap"][:, 0:W], H["ones_row"][0:1, 0:128],
             H["mrow"][0:1, par * 512 + t_lo * 128: par * 512 + t_lo * 128 + W],
             start=False, stop=True, incs=(("sT", 1),))
        if jj == 11:
            st2["mrow_rec"][par] = ("sT", p.n["sT"])
        ew = [("sT", p.n["sT"])]
        if st2["pslot_rec"][jj]:
            ew.append(st2["pslot_rec"][jj])
        pslot = H["pring"][:, jj * 512: jj * 512 + W]
        p.sc("activation", pslot, reg["ap"][:, 0:W], AF.Exp,
             scale=H["cols"][:, 1:2], bias=H["kbcsb"][:, kb_i:kb_i + 1],
             waits=ew, incs=(("sS", 1),))
        p.ps_free(reg, "sS")
        exp_done[jj] = p.n["sS"]
    for jj in range(12):
        t_lo, t_hi = max(0, jj - 8), min(3, jj)
        kb_i = 4 * qg + jj
        for t in range(t_lo, t_hi + 1):
            stt = (t == 0 and jj == 0) or (t == 3 and jj == 3)
            sp = (t == 2 and jj == 10) or (t == 3 and jj == 11)
            p.mm(oreg["ap"][:, OOFF[t]:OOFF[t] + 129],
                 H["pring"][:, jj * 512 + (t - t_lo) * 128:
                             jj * 512 + (t - t_lo) * 128 + 128],
                 H["vtm"][:, (kb_i * NKV + kv) * 130:
                          (kb_i * NKV + kv) * 130 + 129],
                 start=stt, stop=sp,
                 waits=([("sS", exp_done[jj])] + (oreg["waits"] if jj == 0 and t == 0 else []))
                 if t == t_lo else (),
                 incs=(("sT", 1),) if jj == t + 8 else ())
            if jj == t + 8:
                av_done[t] = p.n["sT"]
        st2["pslot_rec"][jj] = ("sT", p.n["sT"])
    for t in range(4):
        off = OOFF[t]
        gate = av_done[2] if t < 3 else av_done[3]
        p.ve("reciprocal", H["mcol"][:, t % 2:t % 2 + 1],
             oreg["ap"][:, off + 128:off + 129],
             waits=[("sT", gate)] + ([st2["rcp_rec"][t % 2]] if st2["rcp_rec"][t % 2] else []),
             incs=(("sV", 1),))
        RCPV = p.n["sV"]
        osl = t % 2
        ow = [("sV", RCPV)]
        if st2["oslot_rec"][osl]:
            ow.append(st2["oslot_rec"][osl])
        otile = H["otmp"][:, osl * 128:(osl + 1) * 128]
        p.sc("mul", otile, oreg["ap"][:, off:off + 128],
             H["mcol"][:, t % 2:t % 2 + 1], waits=ow, incs=(("sS", 1),))
        st2["rcp_rec"][t % 2] = ("sS", p.n["sS"])
        tsl = t % 2
        tw = [("sS", p.n["sS"])]
        if st2["tslot_rec"][tsl]:
            tw.append(st2["tslot_rec"][tsl])
        p.tr(H["psS"][:, 512 + tsl * 128: 512 + (tsl + 1) * 128],
             otile, waits=tw, incs=(("sT", 1),))
        st2["oslot_rec"][osl] = ("sT", p.n["sT"])
        p.ve("tensor_copy",
             H["qhat"][:, h * TOK + q0 + t * 128: h * TOK + q0 + (t + 1) * 128],
             H["psS"][:, 512 + tsl * 128: 512 + (tsl + 1) * 128],
             waits=[("sT", p.n["sT"])], incs=(("sV", 1),))
        st2["tslot_rec"][tsl] = ("sV", p.n["sV"])
    p.ps_free(oreg, "sS")


def phase3(p: Plan):
    H = p.H
    for kc in range(DC):
        p.dma("sy", H["xf32"][:, kc * TOK:(kc + 1) * TOK],
              H["xT"][kc * 128:(kc + 1) * 128, TOK:TT], "sXF",
              waits=[("sT", p.P1["sT"])] if kc == 0 else ())
    XFALL = ("sXF", p.n["sXF"])
    ssum = [p.ps_alloc(1) for _ in range(2)]
    ssq = [p.ps_alloc(1) for _ in range(2)]
    x1_rec = [None, None]
    sq_rec = [None, None]
    x1done = [[None] * 2 for _ in range(DC)]
    pend = []

    def emit_stats(b2, s2b, x1sl, sqsl, sv):
        p.mm(ssum[s2b]["ap"][0:1, 0:512], H["ones_col"][:, :], x1sl,
             start=(b2 == 0), stop=(b2 == DC - 1),
             waits=(ssum[s2b]["waits"] if b2 == 0 else []) + [("sS", sv)],
             incs=(("sT", 1),) if b2 == DC - 1 else ())
        p.mm(ssq[s2b]["ap"][0:1, 0:512], H["ones_col"][:, :], sqsl,
             start=(b2 == 0), stop=(b2 == DC - 1),
             waits=(ssq[s2b]["waits"] if b2 == 0 else []),
             incs=(("sT", 1),) if b2 == DC - 1 else ())
        x1_rec[s2b] = ("sT", p.n["sT"])
        sq_rec[s2b] = ("sT", p.n["sT"])

    for b in range(DC):
        wt = p.wload(H["woT"][b], D)
        for s2 in range(2):
            reg = p.ps_alloc(1)
            for kc in range(DC):
                p.mm(reg["ap"][:, 0:512], wt["ap"][:, kc * 128:(kc + 1) * 128],
                     H["qhat"][:, kc * TOK + s2 * 512: kc * TOK + s2 * 512 + 512],
                     start=(kc == 0), stop=(kc == DC - 1),
                     waits=(reg["waits"] + [wt["ready"], ("sV", p.P2["sV"]),
                             ("sS", p.P2["sS"])])
                     if kc == 0
                     else ([wt["ready"]] if (kc == DC - 1 and s2 == 1) else ()),
                     incs=(("sT", 1),) if kc == DC - 1 else ())
            if s2 == 1:
                p.wfree(wt, p.n["sT"])
            if pend:
                emit_stats(*pend.pop(0))
            xsl = H["xf32"][:, b * TOK + s2 * 512: b * TOK + s2 * 512 + 512]
            p.ve("tensor_add", xsl, reg["ap"][:, 0:512], xsl,
                 waits=[("sT", p.n["sT"]), XFALL], incs=(("sV", 1),), drain=True)
            p.ps_free(reg, "sV")
            x1done[b][s2] = p.n["sV"]
            x1sl = H["x1bf"][:, s2 * 512:(s2 + 1) * 512]
            sqsl = H["sqb3"][:, s2 * 512:(s2 + 1) * 512]
            w1 = [("sV", p.n["sV"])] + ([x1_rec[s2]] if x1_rec[s2] else [])
            if b == 0 and s2 == 0:
                w1 += [("sDbg", p.n["sDbg"])]
            p.sc("copy", x1sl, xsl, waits=w1, incs=(("sS", 1),))
            w2 = [("sV", p.n["sV"])] + ([sq_rec[s2]] if sq_rec[s2] else [])
            p.sc("activation", sqsl, xsl, AF.Square, waits=w2, incs=(("sS", 1),))
            pend.append((b, s2, x1sl, sqsl, p.n["sS"]))
    while pend:
        emit_stats(*pend.pop(0))
    STATS_T = p.n["sT"]
    bregs = []
    A_ = H["srow"][0:1, 0:512]
    B_ = H["srow"][0:1, 512:1024]
    C_ = H["srow"][0:1, 1024:1536]
    for s2 in range(2):
        p.ve("tensor_scalar", A_, ssum[s2]["ap"][0:1, 0:512], 1.0 / D, None,
             OP.mult, waits=[("sT", STATS_T)], incs=(("sV", 1),), drain=True)
        p.ve("tensor_mul", B_, A_, A_, incs=(("sV", 1),), drain=True)
        p.ve("scalar_tensor_tensor", B_, ssq[s2]["ap"][0:1, 0:512], 1.0 / D, B_,
             OP.mult, OP.subtract, incs=(("sV", 1),), drain=True)
        p.sc("activation", C_, B_, AF.Sqrt, bias=H["epsv"][0:1, 1:2],
             waits=[("sV", p.n["sV"])], incs=(("sS", 1),), drain=True)
        p.ve("reciprocal", B_, C_, waits=[("sS", p.n["sS"])], incs=(("sV", 1),),
             drain=True)
        p.ve("tensor_copy", H["lnrowB"][0:1, s2 * 512:(s2 + 1) * 512], B_,
             incs=(("sV", 1),), drain=True)
        p.ve("tensor_mul", C_, A_, B_, incs=(("sV", 1),), drain=True)
        p.ve("tensor_copy", H["lnrowB"][0:1, 1024 + s2 * 512: 1024 + (s2 + 1) * 512],
             C_, incs=(("sV", 1),), drain=True)
        p.ps_free(ssum[s2], "sV")
        p.ps_free(ssq[s2], "sV")
        br = p.ps_alloc(1)
        p.mm(br["ap"][:, 0:512], H["ones_row"][0:1, 0:128],
             H["lnrowB"][0:1, s2 * 512:(s2 + 1) * 512], True, True,
             waits=br["waits"] + [("sV", p.n["sV"])], incs=(("sT", 1),))
        bm = p.ps_alloc(1)
        p.mm(bm["ap"][:, 0:512], H["ones_row"][0:1, 0:128],
             H["lnrowB"][0:1, 1024 + s2 * 512: 1024 + (s2 + 1) * 512], True, True,
             waits=bm["waits"], incs=(("sT", 1),))
        bregs.append((br, bm))
    BC_T = p.n["sT"]
    p.X2DONE = {}
    for b in range(DC):
        for s2 in range(2):
            br, bm = bregs[s2]
            xsl = H["xf32"][:, b * TOK + s2 * 512: b * TOK + s2 * 512 + 512]
            usl = H["x1bf"][:, s2 * 512:(s2 + 1) * 512]
            w1 = [("sT", BC_T)] + ([x1_rec[s2]] if x1_rec[s2] else [])
            p.ve("tensor_mul", usl, xsl, br["ap"][:, 0:512], waits=w1,
                 incs=(("sV", 1),), drain=True)
            x1_rec[s2] = None
            p.ve("tensor_sub", H["x2bf"][:, b * TOK + s2 * 512: b * TOK + s2 * 512 + 512],
                 usl, bm["ap"][:, 0:512], incs=(("sV", 1),))
            x1_rec[s2] = ("sV", p.n["sV"])
            p.X2DONE[(b, s2)] = p.n["sV"]
    for br, bm in bregs:
        p.ps_free(br, "sV")
        p.ps_free(bm, "sV")
    p.tap("d_x2bf", H["x2bf"][:, :], [("sV", p.n["sV"])])
    p.P3 = dict(sV=p.n["sV"], sT=p.n["sT"], sS=p.n["sS"])


def hsl(H, c, tok0, w):
    if c < 32:
        return H["hbufA"][:, c * TOK + tok0: c * TOK + tok0 + w]
    if c < 48:
        return H["hbufB1"][:, (c - 32) * TOK + tok0: (c - 32) * TOK + tok0 + w]
    if c < 56:
        return H["hbufB2"][:, (c - 48) * TOK + tok0: (c - 48) * TOK + tok0 + w]
    return H["hbufB3"][:, (c - 56) * TOK + tok0: (c - 56) * TOK + tok0 + w]


def phase4(p: Plan):
    H = p.H
    GATE = [("sV", p.P3["sV"]), ("sS", p.P3["sS"])]
    first_ps = [True]
    rctx = {}

    def router_group(kt):
        reg = p.ps_alloc(1)
        for kc in range(DC):
            w = []
            if kt == 0:
                w = [("sV", p.X2DONE[(kc, 1)])]
            if kc == 0:
                w = reg["waits"] + w + [("sConst", p.SETUP["sConst"])]
            p.mm(reg["ap"][:, 0:NE],
                 H["x2bf"][:, kc * TOK + kt * 128: kc * TOK + (kt + 1) * 128],
                 H["rwsb"][:, kc * NE:(kc + 1) * NE],
                 start=(kc == 0), stop=(kc == DC - 1), waits=w,
                 incs=(("sT", 1),) if kc == DC - 1 else ())
        sc0 = H["selsb"][:, 0:8]
        p.sc("activation", sc0, reg["ap"][:, 0:NE], AF.Sigmoid,
             waits=[("sT", p.n["sT"])], incs=(("sS", 1),))
        p.ps_free(reg, "sS")
        sel = H["selsb"][:, 8:16]
        p.ve("tensor_add", sel, sc0, H["ebsb"][:, :],
             waits=[("sS", p.n["sS"]), ("sConst", p.SETUP["sConst"])], incs=(("sV", 1),), drain=True)
        p.ve("tensor_reduce", H["rtrows"][:, 0:1], sel, AX.X, OP.max, incs=(("sV", 1),), drain=True)
        eq = H["selsb"][:, 16:24]
        p.ve("tensor_scalar", eq, sel, H["rtrows"][:, 0:1], None, OP.is_equal,
             incs=(("sV", 1),), drain=True)
        p.ve("scalar_tensor_tensor", eq, eq, -1e9, sel, OP.mult, OP.add,
             incs=(("sV", 1),), drain=True)
        p.ve("tensor_reduce", H["rtrows"][:, 1:2], eq, AX.X, OP.max, incs=(("sV", 1),), drain=True)
        tm = H["selsb"][:, 24:32]
        p.ve("tensor_scalar", tm, sel, H["rtrows"][:, 1:2], None, OP.is_ge,
             incs=(("sV", 1),), drain=True)
        p.ve("tensor_mul", tm, tm, sc0, incs=(("sV", 1),), drain=True)
        p.ve("tensor_reduce", H["rtrows"][:, 2:3], tm, AX.X, OP.add, incs=(("sV", 1),), drain=True)
        p.ve("tensor_scalar", H["rtrows"][:, 2:3], H["rtrows"][:, 2:3], 1e-9, None,
             OP.add, incs=(("sV", 1),), drain=True)
        p.ve("reciprocal", H["rtrows"][:, 3:4], H["rtrows"][:, 2:3], incs=(("sV", 1),), drain=True)
        p.ve("tensor_scalar", H["cmb8"][:, :], tm, H["rtrows"][:, 3:4], None, OP.mult,
             incs=(("sV", 1),), drain=True)
        rctx[kt] = p.n["sV"]

    def router_fin(kt):
        tw = [("sV", rctx.pop(kt))]
        if first_ps[0]:
            tw.append(("sS", p.P3["sS"]))
            first_ps[0] = False
        p.tr(H["psS"][0:8, 0:128], H["cmb8"][:, :], waits=tw, incs=(("sT", 1),))
        p.ve("tensor_copy", H["combsb"][0:8, kt * 128:(kt + 1) * 128],
             H["psS"][0:8, 0:128], waits=[("sT", p.n["sT"])], incs=(("sV", 1),))

    silu_rec = [None, None]
    first_h = True
    for c in range(HS // 128):
        if c < 8:
            router_group(c)
        w1t = p.wload(H["sw1T"][c], D)
        w2t = p.wload(H["sw2T"][c], D)
        for s2 in range(2):
            regg = p.ps_alloc(1)
            for kc in range(DC):
                w = []
                if c == 0:
                    w = [("sV", p.X2DONE[(kc, s2)])]
                if kc == 0:
                    w = regg["waits"] + w + [w1t["ready"]]
                elif kc == DC - 1 and s2 == 1:
                    w = w + [w1t["ready"]]
                p.mm(regg["ap"][:, 0:512], w1t["ap"][:, kc * 128:(kc + 1) * 128],
                     H["x2bf"][:, kc * TOK + s2 * 512: kc * TOK + s2 * 512 + 512],
                     start=(kc == 0), stop=(kc == DC - 1), waits=w,
                     incs=(("sT", 1),) if kc == DC - 1 else ())
            GT = p.n["sT"]
            if s2 == 1:
                p.wfree(w1t, GT)
            regu = p.ps_alloc(1)
            for kc in range(DC):
                p.mm(regu["ap"][:, 0:512], w2t["ap"][:, kc * 128:(kc + 1) * 128],
                     H["x2bf"][:, kc * TOK + s2 * 512: kc * TOK + s2 * 512 + 512],
                     start=(kc == 0), stop=(kc == DC - 1),
                     waits=(regu["waits"] + [w2t["ready"]]) if kc == 0
                     else ([w2t["ready"]] if (kc == DC - 1 and s2 == 1) else ()),
                     incs=(("sT", 1),) if kc == DC - 1 else ())
            if s2 == 1:
                p.wfree(w2t, p.n["sT"])
            ssl = H["silub"][:, s2 * 512:(s2 + 1) * 512]
            sw = [("sT", GT)] + ([silu_rec[s2]] if silu_rec[s2] else [])
            if first_h:
                sw += GATE
            p.sc("activation", ssl, regg["ap"][:, 0:512], AF.Silu,
                 waits=sw, incs=(("sS", 1),))
            p.ps_free(regg, "sS")
            hw = [("sS", p.n["sS"]), ("sT", p.n["sT"])]
            if first_h:
                hw += [("sV", p.P3["sV"]), ("sDbg", p.n["sDbg"])]
                first_h = False
            p.ve("tensor_mul", hsl(H, c, s2 * 512, 512), ssl, regu["ap"][:, 0:512],
                 waits=hw, incs=(("sV", 1),))
            silu_rec[s2] = ("sV", p.n["sV"])
            p.ps_free(regu, "sV")
        if c < 8:
            router_fin(c)
    HDONE = p.n["sV"]
    p.tap("d_comb", H["combsb"][:, :], [("sV", p.n["sV"])])
    p.tap("d_hbuf", H["hbufA"][:, :], [("sV", p.n["sV"])])

    so_gen = [0, 0]
    for b in range(DC):
        wta = p.wload(H["sw3T"][b][:, 0:4096], 4096)
        wtb = p.wload(H["sw3T"][b][:, 4096:8192], 4096)
        for s2 in range(2):
            reg = p.ps_alloc(1)
            for hc in range(64):
                wt = wta if hc < 32 else wtb
                w = []
                if hc == 0:
                    w = reg["waits"] + [wta["ready"], ("sV", HDONE)]
                elif hc == 32:
                    w = [wtb["ready"]]
                if s2 == 1 and hc == 31:
                    w = w + [wta["ready"]]
                if s2 == 1 and hc == 63:
                    w = w + [wtb["ready"]]
                p.mm(reg["ap"][:, 0:512],
                     wt["ap"][:, (hc % 32) * 128:((hc % 32) + 1) * 128],
                     hsl(H, hc, s2 * 512, 512),
                     start=(hc == 0), stop=(hc == 63), waits=w,
                     incs=(("sT", 1),) if hc in (31, 63) else ())
            Y3STOP = p.n["sT"]
            if s2 == 1:
                p.wfree(wta, Y3STOP - 1)
                p.wfree(wtb, Y3STOP)
            slot = s2
            sw = [("sT", p.n["sT"]), (f"sSp{slot}", 16 * so_gen[slot])]
            p.ve("tensor_copy", H["sptmpSO"][:, slot * 512:(slot + 1) * 512],
                 reg["ap"][:, 0:512], waits=sw, incs=(("sV", 1),))
            p.ps_free(reg, "sV")
            p.dma("sy", H["spso"][b * 128:(b + 1) * 128, s2 * 512:(s2 + 1) * 512],
                  H["sptmpSO"][:, slot * 512:(slot + 1) * 512], f"sSp{slot}",
                  waits=[("sV", p.n["sV"])])
            so_gen[slot] += 1
    p.P4 = dict(sV=p.n["sV"], sT=p.n["sT"], sS=p.n["sS"],
                sSp0=p.n["sSp0"], sSp1=p.n["sSp1"])


def phase5(p: Plan):
    H = p.H
    Y3S_T = p.P4["sT"]
    silu_rec = [None, None]
    t_rec = [None, None]
    first_h = True
    for e in range(NE):
        cregs = []
        for s2 in range(2):
            cr = p.ps_alloc(1)
            p.mm(cr["ap"][:, 0:512], H["esel"][0:8, e * 128:(e + 1) * 128],
                 H["combsb"][0:8, s2 * 512:(s2 + 1) * 512], True, True,
                 waits=cr["waits"] + ([("sV", p.P4["sV"])] if e == 0 and s2 == 0 else []),
                 incs=(("sT", 1),))
            cregs.append(cr)
        for hc in range(8):
            w1t = p.wload(H["w12T"][e][hc], D)
            w2t = p.wload(H["w12T"][e][hc + 8], D)
            for s2 in range(2):
                reg1 = p.ps_alloc(1)
                for kc in range(DC):
                    p.mm(reg1["ap"][:, 0:512], w1t["ap"][:, kc * 128:(kc + 1) * 128],
                         H["x2bf"][:, kc * TOK + s2 * 512: kc * TOK + s2 * 512 + 512],
                         start=(kc == 0), stop=(kc == DC - 1),
                         waits=(reg1["waits"] + [w1t["ready"]]) if kc == 0
                         else ([w1t["ready"]] if (kc == DC - 1 and s2 == 1) else ()),
                         incs=(("sT", 1),) if kc == DC - 1 else ())
                H1T = p.n["sT"]
                if s2 == 1:
                    p.wfree(w1t, H1T)
                reg2 = p.ps_alloc(1)
                for kc in range(DC):
                    p.mm(reg2["ap"][:, 0:512], w2t["ap"][:, kc * 128:(kc + 1) * 128],
                         H["x2bf"][:, kc * TOK + s2 * 512: kc * TOK + s2 * 512 + 512],
                         start=(kc == 0), stop=(kc == DC - 1),
                         waits=(reg2["waits"] + [w2t["ready"]]) if kc == 0
                         else ([w2t["ready"]] if (kc == DC - 1 and s2 == 1) else ()),
                         incs=(("sT", 1),) if kc == DC - 1 else ())
                if s2 == 1:
                    p.wfree(w2t, p.n["sT"])
                ssl = H["silub"][:, s2 * 512:(s2 + 1) * 512]
                sw = [("sT", H1T)] + ([silu_rec[s2]] if silu_rec[s2] else [])
                p.sc("activation", ssl, reg1["ap"][:, 0:512], AF.Silu,
                     waits=sw, incs=(("sS", 1),))
                p.ps_free(reg1, "sS")
                tsl = H["rltmpSO"][:, s2 * 512:(s2 + 1) * 512]
                tw = [("sS", p.n["sS"]), ("sT", p.n["sT"])] +                      ([t_rec[s2]] if t_rec[s2] else [])
                p.ve("tensor_mul", tsl, ssl, reg2["ap"][:, 0:512],
                     waits=tw, incs=(("sV", 1),))
                silu_rec[s2] = ("sV", p.n["sV"])
                p.ps_free(reg2, "sV")
                hw = [("sV", p.n["sV"])]
                if first_h:
                    hw += [("sT", Y3S_T), ("sDbg", p.n["sDbg"])]
                    first_h = False
                p.ve("tensor_mul", hsl(H, e * 8 + hc, s2 * 512, 512), tsl,
                     cregs[s2]["ap"][:, 0:512], waits=hw, incs=(("sV", 1),))
                t_rec[s2] = ("sV", p.n["sV"])
        for cr in cregs:
            p.ps_free(cr, "sV")
    HRDONE = p.n["sV"]

    ssum = [p.ps_alloc(1) for _ in range(2)]
    ssq = [p.ps_alloc(1) for _ in range(2)]
    rl_gen = [0, 0]
    rl_rec = [None, None]
    sq_rec = [None, None]
    x3_rec = [None, None]
    pend = []

    def emit_stats(b2, s2b, x2sl, sqsl, sv):
        p.mm(ssum[s2b]["ap"][0:1, 0:512], H["ones_col"][:, :], x2sl,
             start=(b2 == 0), stop=(b2 == DC - 1),
             waits=(ssum[s2b]["waits"] if b2 == 0 else []) + [("sS", sv)],
             incs=(("sT", 1),) if b2 == DC - 1 else ())
        p.mm(ssq[s2b]["ap"][0:1, 0:512], H["ones_col"][:, :], sqsl,
             start=(b2 == 0), stop=(b2 == DC - 1),
             waits=(ssq[s2b]["waits"] if b2 == 0 else []),
             incs=(("sT", 1),) if b2 == DC - 1 else ())
        x3_rec[s2b] = ("sT", p.n["sT"])
        sq_rec[s2b] = ("sT", p.n["sT"])

    for b in range(DC):
        wta = p.wload(H["w3T"][b][0:4].rearrange("e p h -> p e h"), 4096, nshape=(4, HR))
        wtb = p.wload(H["w3T"][b][4:8].rearrange("e p h -> p e h"), 4096, nshape=(4, HR))
        for s2 in range(2):
            slot = s2
            rw = [("sSp0", p.P4["sSp0"]), ("sSp1", p.P4["sSp1"])] \
                if (b == 0 and s2 == 0) else []
            if rl_rec[slot]:
                rw = rw + [rl_rec[slot]]
            if b == 0:
                rw = rw + [("sV", HRDONE)]
            p.dma("gp", H["rltmpSO"][:, slot * 512:(slot + 1) * 512],
                  H["spso"][b * 128:(b + 1) * 128, s2 * 512:(s2 + 1) * 512],
                  f"sR{slot}", waits=rw)
            rl_gen[slot] += 1
            RLRDY = (f"sR{slot}", 16 * rl_gen[slot])
            reg = p.ps_alloc(1)
            nmm = 0
            for e in range(NE):
                wt = wta if e < 4 else wtb
                for hc in range(8):
                    w = []
                    if nmm == 0:
                        w = reg["waits"] + [wta["ready"], ("sV", HRDONE)]
                    elif e == 4 and hc == 0:
                        w = [wtb["ready"]]
                    if s2 == 1 and nmm == 31:
                        w = w + [wta["ready"]]
                    if s2 == 1 and nmm == 63:
                        w = w + [wtb["ready"]]
                    p.mm(reg["ap"][:, 0:512],
                         wt["ap"][:, (e % 4) * 1024 + hc * 128:
                                  (e % 4) * 1024 + hc * 128 + 128],
                         hsl(H, e * 8 + hc, s2 * 512, 512),
                         start=(nmm == 0), stop=(nmm == 63), waits=w,
                         incs=(("sT", 1),) if nmm in (31, 63) else ())
                    nmm += 1
            Y3STOP = p.n["sT"]
            if s2 == 1:
                p.wfree(wta, Y3STOP - 1)
                p.wfree(wtb, Y3STOP)
            if pend:
                emit_stats(*pend.pop(0))
            spw = [("sSp0", p.P4["sSp0"]), ("sSp1", p.P4["sSp1"])] \
                if (b == 0 and s2 == 0) else []
            tsl = H["sqtmp5"][:, s2 * 512:(s2 + 1) * 512]
            p.ve("tensor_add", tsl, reg["ap"][:, 0:512],
                 H["rltmpSO"][:, slot * 512:(slot + 1) * 512],
                 waits=[("sT", Y3STOP), RLRDY] + spw, incs=(("sV", 1),),
                 drain=True)
            rl_rec[slot] = ("sV", p.n["sV"])
            p.ps_free(reg, "sV")
            x2sl = H["x2bf"][:, b * TOK + s2 * 512: b * TOK + s2 * 512 + 512]
            p.ve("tensor_add", x2sl, tsl, x2sl, incs=(("sV", 1),), drain=True)
            X3V = p.n["sV"]
            sqsl = H["sqtmp5"][:, (2 + s2) * 512:(3 + s2) * 512]
            qw = [("sV", X3V)] + ([sq_rec[s2]] if sq_rec[s2] else [])
            p.sc("activation", sqsl, x2sl, AF.Square, waits=qw, incs=(("sS", 1),))
            pend.append((b, s2, x2sl, sqsl, p.n["sS"]))
    while pend:
        emit_stats(*pend.pop(0))
    STATS_T = p.n["sT"]
    p.tap("d_x3bf", H["x2bf"][:, :], [("sV", p.n["sV"])])

    bregs = []
    A_ = H["srow"][0:1, 0:512]
    B_ = H["srow"][0:1, 512:1024]
    C_ = H["srow"][0:1, 1024:1536]
    for s2 in range(2):
        spw = [("sSp0", p.P4["sSp0"]), ("sSp1", p.P4["sSp1"])] if s2 == 0 else []
        p.ve("tensor_scalar", A_, ssum[s2]["ap"][0:1, 0:512], 1.0 / D, None,
             OP.mult, waits=[("sT", STATS_T)] + spw, incs=(("sV", 1),), drain=True)
        p.ve("tensor_mul", B_, A_, A_, incs=(("sV", 1),), drain=True)
        p.ve("scalar_tensor_tensor", B_, ssq[s2]["ap"][0:1, 0:512], 1.0 / D, B_,
             OP.mult, OP.subtract, incs=(("sV", 1),), drain=True)
        p.sc("activation", C_, B_, AF.Sqrt, bias=H["epsv"][0:1, 1:2],
             waits=[("sV", p.n["sV"])], incs=(("sS", 1),), drain=True)
        p.ve("reciprocal", B_, C_, waits=[("sS", p.n["sS"])], incs=(("sV", 1),),
             drain=True)
        p.ve("tensor_copy", H["lnrowB"][0:1, s2 * 512:(s2 + 1) * 512], B_,
             incs=(("sV", 1),), drain=True)
        p.ve("tensor_mul", C_, A_, B_, incs=(("sV", 1),), drain=True)
        p.ve("tensor_copy", H["lnrowB"][0:1, 1024 + s2 * 512: 1024 + (s2 + 1) * 512],
             C_, incs=(("sV", 1),), drain=True)
        p.ps_free(ssum[s2], "sV")
        p.ps_free(ssq[s2], "sV")
    br = p.ps_alloc(2)
    bm = p.ps_alloc(2)
    for s2 in range(2):
        p.mm(br["ap"][:, s2 * 512:(s2 + 1) * 512], H["ones_row"][0:1, 0:128],
             H["lnrowB"][0:1, s2 * 512:(s2 + 1) * 512], True, True,
             waits=(br["waits"] + [("sV", p.n["sV"])]) if s2 == 0 else (),
             incs=(("sT", 1),))
        p.mm(bm["ap"][:, s2 * 512:(s2 + 1) * 512], H["ones_row"][0:1, 0:128],
             H["lnrowB"][0:1, 1024 + s2 * 512: 1024 + (s2 + 1) * 512], True, True,
             waits=bm["waits"] if s2 == 0 else (), incs=(("sT", 1),))
    bregs = [(br, bm)]
    BC_T = p.n["sT"]

    u_rec = None
    out_gen = [0, 0]
    for b in range(DC):
        x2sl = H["x2bf"][:, b * TOK: (b + 1) * TOK]
        usl = H["sqtmp5"][:, 0:1024]
        w1 = [("sT", BC_T)] + ([u_rec] if u_rec else [])
        p.ve("tensor_mul", usl, x2sl, br["ap"][:, 0:1024], waits=w1,
             incs=(("sV", 1),), drain=True)
        slot = b % 2
        dst = H["outtmp"][:, 0:1024] if slot == 0 else H["outtmp2"][:, 0:1024]
        ow = [(f"sO{slot}", 16 * out_gen[slot])] if out_gen[slot] else []
        p.ve("tensor_sub", dst, usl, bm["ap"][:, 0:1024], waits=ow,
             incs=(("sV", 1),))
        u_rec = ("sV", p.n["sV"])
        p.dma("sy", H["out"][b * 128:(b + 1) * 128, 0:TOK], dst,
              f"sO{slot}", waits=[("sV", p.n["sV"])])
        out_gen[slot] += 1
    for br, bm in bregs:
        p.ps_free(br, "sV")
        p.ps_free(bm, "sV")


def build_nc(dbg=()):
    nc = bass.Bass("TRN2", target_bir_lowering=False)
    es = contextlib.ExitStack()
    H = declare(nc, dbg=dbg)
    sems = {name: es.enter_context(nc.semaphore(name)) for name in SEM_NAMES}
    plans = []
    with nc.Block() as block:
        def run(tag):
            pl = Plan(nc, tag, H, sems)
            with nc.allow_low_precision("bf16 matmul-operand pipeline by design"):
                walk(pl)
            plans.append(pl)
        block.gpsimd(lambda g: run("gp"))
        block.sync(lambda s: run("sy"))
        block.tensor(lambda t: run("te"))
        block.vector(lambda v: run("ve"))
        block.scalar(lambda s: run("sc"))
    for pl in plans[1:]:
        assert pl.n == plans[0].n, "walk counters diverged between engines"
    return nc


def host_tile_w(W):
    Out, In = W.shape
    Ob, Kc = Out // 128, In // 128
    return np.ascontiguousarray(
        W.reshape(Ob, 128, Kc, 128).transpose(0, 3, 2, 1).reshape(Ob, 128, In),
        dtype=np.float32)


def make_inmaps(inp):
    x = np.asarray(inp["L0_attn_x"], dtype=np.float32)[0]
    wq = np.asarray(inp["L0_attn_wq"], np.float32)
    wk = np.asarray(inp["L0_attn_wk"], np.float32)
    wv = np.asarray(inp["L0_attn_wv"], np.float32)
    wo = np.asarray(inp["L0_attn_wo"], np.float32)
    qn = np.asarray(inp["L0_attn_q_norm_w"], np.float32)
    kn = np.asarray(inp["L0_attn_k_norm_w"], np.float32)
    w12 = np.asarray(inp["L0_ffn_w12"], np.float32)
    w3 = np.asarray(inp["L0_ffn_w3"], np.float32)
    rw = np.asarray(inp["L0_ffn_router_w"], np.float32)
    eb = np.asarray(inp["L0_ffn_expert_bias"], np.float32)
    sw1 = np.asarray(inp["L0_ffn_sw1"], np.float32)
    sw2 = np.asarray(inp["L0_ffn_sw2"], np.float32)
    sw3 = np.asarray(inp["L0_ffn_sw3"], np.float32)

    com = {}
    com["wqT"] = host_tile_w(wq)
    com["wkT"] = host_tile_w(wk)
    com["wvT"] = np.ascontiguousarray(wv.T)
    com["woT"] = host_tile_w(wo)
    com["rwT"] = np.ascontiguousarray(rw.T)
    com["sw1T"] = host_tile_w(sw1)
    com["sw2T"] = host_tile_w(sw2)
    com["sw3T"] = host_tile_w(sw3)
    com["w12T"] = np.stack([host_tile_w(w12[e]) for e in range(NE)])
    w3t = np.stack([host_tile_w(w3[e]) for e in range(NE)])
    com["w3T"] = np.ascontiguousarray(w3t.transpose(1, 0, 2, 3))
    com["qnw"] = qn.reshape(128, 1)
    com["knw"] = kn.reshape(128, 1)
    com["ebias"] = np.ascontiguousarray(np.broadcast_to(eb[None, :], (128, NE)))

    import jax
    cpu = jax.devices("cpu")[0]
    with jax.default_device(cpu):
        import jax.numpy as jnp
        _inv = 1.0 / 10000.0 ** (jnp.arange(0, HD, 2, dtype=jnp.float32) / HD)
        _pos = jnp.arange(0, S, dtype=jnp.int32)
        _ang = _pos[:, None].astype(jnp.float32) * _inv[None, :]
        _cosg = np.asarray(jnp.cos(_ang), np.float32)
        _sing = np.asarray(jnp.sin(_ang), np.float32)
    maps = []
    for c in range(8):
        m = dict(com)
        own = x[c * TOK:(c + 1) * TOK]
        halo = x[c * TOK - TOK: c * TOK] if c > 0 else np.zeros((TOK, D), np.float32)
        m["xT"] = np.ascontiguousarray(np.concatenate([halo, own], 0).T)
        lo = c * TOK - TOK
        csl = np.zeros((TT, HD // 2), np.float32)
        ssl = np.zeros((TT, HD // 2), np.float32)
        n0 = max(0, -lo)
        csl[n0:] = _cosg[max(lo, 0):lo + TT]
        ssl[n0:] = _sing[max(lo, 0):lo + TT]
        csl[:n0] = _cosg[0]
        ssl[:n0] = _sing[0]
        m["tabCin"] = np.ascontiguousarray(np.concatenate([csl.T, csl.T], 0))
        m["tabSin"] = np.ascontiguousarray(np.concatenate([-ssl.T, ssl.T], 0))
        kbc = np.zeros((128, TT // 128), np.float32)
        if c == 0:
            kbc[:, :TOK // 128] = NEG
        m["kbc"] = kbc
        maps.append(m)
    return maps


from concourse.bass_utils import run_bass_kernel_spmd


def kernel(**inputs):
    maps = make_inmaps(inputs)
    nc = build_nc()
    res = run_bass_kernel_spmd(nc, maps, core_ids=list(range(8)))
    parts = [np.ascontiguousarray(np.asarray(res.results[c]["out"]).T)
             for c in range(8)]
    out = np.concatenate(parts, axis=0).reshape(1, S, D).astype(np.float32)
    return out
